# version 07
# baseline (speedup 1.0000x reference)
import numpy as np
import ml_dtypes
import concourse.bass as bass
import concourse.mybir as mybir
from concourse.bass_utils import run_bass_kernel_spmd

F32 = mybir.dt.float32
BF16 = mybir.dt.bfloat16
AF = mybir.ActivationFunctionType
ALU = mybir.AluOpType
AX = mybir.AxisListType

H = 4096
CH = 32
TOK = 1024
TT = 512
NTB = 2
KVH = 8
QH = 32
IT = 86
EPS_LN = 1.0e-5
EPS_ATT = 1.0e-10
CCW = KVH * 128 + KVH

ENGS = ("sync", "gpsimd", "tensor", "scalar", "vector")


class Prog:
    def __init__(self):
        self.q = {e: [] for e in ENGS}
        self.hi = {e: {} for e in ENGS}

    def add(self, eng, fn):
        self.q[eng].append(fn)

    def wait(self, eng, sem, val):
        if val <= 0:
            return
        key = id(sem)
        if self.hi[eng].get(key, -1) >= val:
            return
        self.hi[eng][key] = val
        self.q[eng].append(lambda e, s=sem, v=val: e.wait_ge(s, v))

    def emit(self, block):
        for name in ENGS:
            ops = self.q[name]
            if not ops:
                continue
            getattr(block, name)(lambda e, ops=ops: [f(e) for f in ops])


class Ctr:
    def __init__(self):
        self.v = 0

    def inc(self, n=1):
        self.v += n
        return self.v


def build(upto="b4", dumps=()):
    nc = bass.Bass("TRN2", target_bir_lowering=False, debug=False)

    xT = nc.dram_tensor("xT", [H, TOK], F32, kind="ExternalInput").ap()
    wimg = {}
    for nm, nblk, wide in (("wq", QH, H), ("wk", KVH, H), ("wv", KVH, H),
                           ("wo", CH, H), ("wg", IT, H), ("wu", IT, H),
                           ("wdob", CH // 4, IT * 512)):
        wimg[nm] = nc.dram_tensor(nm, [nblk, 128, wide], F32, kind="ExternalInput").ap()
    bvecs = {}
    for nm, w in (("bq", QH), ("bk", KVH), ("bv", KVH), ("bo", CH),
                  ("ag", IT), ("au", IT), ("ad", CH)):
        bvecs[nm] = nc.dram_tensor(nm, [128, w], F32, kind="ExternalInput").ap()
    ident_in = nc.dram_tensor("ident", [128, 128], BF16, kind="ExternalInput").ap()
    outT = nc.dram_tensor("outT", [H, TOK], F32, kind="ExternalOutput").ap()
    x2d = nc.dram_tensor("x2d", [H, TOK], F32, kind="Internal").ap()
    x2dbf = nc.dram_tensor("x2dbf", [H, TOK], BF16, kind="Internal").ap()
    hd = nc.dram_tensor("hd", [IT * 128, TOK], BF16, kind="Internal").ap()
    cc_in = nc.dram_tensor("cc_in", [128, CCW], F32, kind="Internal").ap()
    cc_out = nc.dram_tensor("cc_out", [128, CCW], F32, kind="Internal").ap()

    sem_ms = nc.alloc_semaphore("s_ms")
    sem_vs = nc.alloc_semaphore("s_vs")
    sem_ss = nc.alloc_semaphore("s_ss")
    sem_cs = nc.alloc_semaphore("s_cs")
    MS, VS, SS = Ctr(), Ctr(), Ctr()
    semG = [nc.alloc_semaphore(f"s_g{i}") for i in range(4)]
    ctrG = [Ctr() for _ in range(4)]
    semX = [nc.alloc_semaphore(f"s_x{i}") for i in range(4)]
    ctrX = [Ctr() for _ in range(4)]
    semO = [nc.alloc_semaphore(f"s_o{i}") for i in range(2)]
    ctrO = [Ctr() for _ in range(2)]
    semA = [nc.alloc_semaphore(f"s_a{i}") for i in range(4)]
    ctrA = [Ctr() for _ in range(4)]
    semB = [nc.alloc_semaphore(f"s_b{i}") for i in range(2)]
    ctrB = [Ctr() for _ in range(2)]
    semW = [nc.alloc_semaphore(f"s_w{i}") for i in range(4)]
    ctrW = [Ctr() for _ in range(4)]
    semD = [nc.alloc_semaphore(f"s_d{i}") for i in range(4)]
    ctrD = [Ctr() for _ in range(4)]

    ident_sb = nc.alloc_sbuf_tensor("ident_sb", [128, 128], BF16)
    ones_col = nc.alloc_sbuf_tensor("ones_col", [128, 16], BF16)
    ones_row = nc.alloc_sbuf_tensor("ones_row", [1, 128], BF16)
    b_sb = {nm: nc.alloc_sbuf_tensor(nm + "_sb", [128, w], F32)
            for nm, w in (("bq", QH), ("bk", KVH), ("bv", KVH), ("bo", CH),
                          ("ag", IT), ("au", IT), ("ad", CH))}
    ln2a = [nc.alloc_sbuf_tensor(f"ln2a{t}", [1, TT], BF16) for t in range(NTB)]
    ln2b = [nc.alloc_sbuf_tensor(f"ln2b{t}", [1, TT], BF16) for t in range(NTB)]
    sc_t = nc.alloc_sbuf_tensor("sc_t", [128, 2 * TT], F32)
    sc_e = nc.alloc_sbuf_tensor("sc_e", [128, 2 * TT], BF16)
    sc_r = nc.alloc_sbuf_tensor("sc_r", [128, 2 * TT], BF16)
    sc_bf = nc.alloc_sbuf_tensor("sc_bf", [128, 2 * TT], BF16)
    lnA_sb = nc.alloc_sbuf_tensor("lnA_sb", [128, TT], BF16)
    lnB_sb = nc.alloc_sbuf_tensor("lnB_sb", [128, TT], BF16)
    sc_o = nc.alloc_sbuf_tensor("sc_o", [128, 2 * TT], F32)
    xre = nc.alloc_sbuf_tensor("xre", [128, 4 * TT], F32)
    wstream = nc.alloc_sbuf_tensor("wstream", [128, IT * 128], BF16)
    kvt_bf = nc.alloc_sbuf_tensor("kvt_bf", [128, KVH * 128], BF16)
    rn_sb = nc.alloc_sbuf_tensor("rn_sb", [128, KVH], F32)
    rn_tmp = nc.alloc_sbuf_tensor("rn_tmp", [128, KVH], F32)
    x2a = nc.alloc_sbuf_tensor("x2a", [128, 4 * TT], BF16)

    TOP = 229344
    MB_BYTES = 88064
    MA_BYTES = 65536
    MB0 = (TOP - MB_BYTES) // 32 * 32
    MA0 = (MB0 - MA_BYTES) // 32 * 32
    A = lambda name, shape, dt, off: nc.alloc_sbuf_tensor_at(name, shape, dt, offset=off)
    xln1 = A("xln1", [128, CH * TOK], BF16, MA0)
    xln2 = A("xln2", [128, CH * TOK], BF16, MA0)
    gu = A("gu", [128, 4 * H], BF16, MB0)
    wdf = A("wdf", [128, 4 * 512], F32, MB0)
    hr = A("hr", [128, 4 * TOK], BF16, MB0 + 32768)
    wds = A("wds", [128, 4 * 512], BF16, MB0 + 40960)
    dn_t = A("dn_t", [128, 2 * TOK], F32, MB0 + 45056)
    dn_o = A("dn_o", [128, 2 * TOK], F32, MB0 + 53248)
    lnA2 = A("lnA2", [128, TOK], BF16, MB0 + 61440)
    lnB2 = A("lnB2", [128, TOK], BF16, MB0 + 63488)
    scbf2 = A("scbf2", [128, 2 * TOK], BF16, MB0 + 65536)
    scr2 = A("scr2", [128, 2 * TOK], BF16, MB0 + 69632)
    sce2 = A("sce2", [128, 2 * TOK], BF16, MB0 + 73728)
    xbf_full = A("xbf_full", [128, CH * TT], BF16, MB0)
    xsq_r = A("xsq_r", [128, 4 * TT], BF16, MB0 + 32768)
    kphi = A("kphi", [128, KVH * TOK], BF16, MB0)
    vbf = A("vbf", [128, KVH * TOK], BF16, MB0 + 16384)
    kt_r = A("kt_r", [128, 4 * 128], BF16, MB0 + 32768)
    vt_r = A("vt_r", [128, 4 * 128], BF16, MB0 + 33792)
    qphi = A("qphi", [128, QH * TOK], BF16, MB0)
    row_mean = A("row_mean", [1, TT], F32, MB0 + 65536)
    row_var = A("row_var", [1, TT], F32, MB0 + 67584)
    row_r = A("row_r", [1, TT], F32, MB0 + 69632)
    row_a_bf = A("row_a_bf", [1, TT], BF16, MB0 + 71680)
    row_b_bf = A("row_b_bf", [1, TT], BF16, MB0 + 72704)
    ccin_sb = A("ccin_sb", [128, CCW], F32, MB0 + 73728)
    ccout_sb = A("ccout_sb", [128, CCW], F32, MB0 + 77888)
    x2bf = A("x2bf", [128, 2 * TT], BF16, MB0 + 82048)
    x2sq = A("x2sq", [128, TT], BF16, MB0 + 84096)
    row_b = A("row_b", [1, TT], F32, MB0 + 85120)

    assert nc.sbuf_base <= MA0, (nc.sbuf_base, MA0)

    def sl(t, i, w):
        return t[:, i * w:(i + 1) * w]

    free = {}
    ready = {}
    xre_ctr = Ctr()

    def dma(P, eng, sems, ctrs, idx, dst, src):
        P.add(eng, lambda e, d=dst, s=src, sm=sems[idx]: e.dma_start(d, s).then_inc(sm, 16))
        return (sems[idx], ctrs[idx].inc(16))

    P = Prog()
    _b0_target = ctrX[0].v + 8 * 16
    def b0_sync(sy, v_target=_b0_target):
        sy.dma_start(ident_sb[:], ident_in[:]).then_inc(semX[0], 16)
        for nm in ("bq", "bk", "bv", "bo", "ag", "au", "ad"):
            sy.dma_start(b_sb[nm][:], bvecs[nm][:]).then_inc(semX[0], 16)
        sy.wait_ge(semX[0], v_target)
    P.add("sync", b0_sync)
    ctrX[0].inc(8 * 16)
    P.add("vector", lambda v: v.memset(ones_col[:], 1.0))
    P.add("vector", lambda v: v.memset(ones_row[:], 1.0))
    with nc.Block() as b0:
        P.emit(b0)

    def ln_finalize(P, s1_ps, s2_ps, a_dst, b_dst, stats_done_ms):
        inv = 1.0 / H
        P.wait("vector", sem_ms, stats_done_ms)
        P.add("vector", lambda v: v.tensor_scalar(row_mean[:], s1_ps[:], inv, None, ALU.mult))
        P.add("vector", lambda v: v.tensor_scalar(row_var[:], s2_ps[:], inv, None, ALU.mult)
              .then_inc(sem_vs, 1))
        v1 = VS.inc()
        P.wait("vector", sem_vs, v1)
        P.add("vector", lambda v: v.tensor_tensor(row_r[:], row_mean[:], row_mean[:], ALU.mult)
              .then_inc(sem_vs, 1))
        v2 = VS.inc()
        P.wait("vector", sem_vs, v2)
        P.add("vector", lambda v: v.tensor_tensor(row_var[:], row_var[:], row_r[:], ALU.subtract)
              .then_inc(sem_vs, 1))
        v_var = VS.inc()
        P.wait("scalar", sem_vs, v_var)
        P.add("scalar", lambda s: s.activation(row_b[:], row_var[:], AF.Ln,
                                               bias=np.float32(EPS_LN), scale=1.0)
              .then_inc(sem_ss, 1))
        s_ln = SS.inc()
        P.wait("scalar", sem_ss, s_ln)
        P.add("scalar", lambda s: s.activation(row_r[:], row_b[:], AF.Exp, bias=0.0, scale=-0.5)
              .then_inc(sem_ss, 1))
        s_r = SS.inc()
        P.wait("vector", sem_ss, s_r)
        P.add("vector", lambda v: v.scalar_tensor_tensor(row_b[:], row_mean[:], -1.0, row_r[:],
                                                         ALU.mult, ALU.mult).then_inc(sem_vs, 1))
        v4 = VS.inc()
        P.add("vector", lambda v, a=a_dst: v.tensor_copy(a[:], row_r[:]))
        P.wait("vector", sem_vs, v4)
        P.add("vector", lambda v, b=b_dst: v.tensor_copy(b[:], row_b[:]).then_inc(sem_vs, 1))
        return VS.inc()

    P = Prog()
    with (
        nc.psum_tensor("ps_s1", [1, TT], F32) as ps_s1,
        nc.psum_tensor("ps_s2", [1, TT], F32) as ps_s2,
        nc.psum_tensor("ps_abc", [128, TT], F32) as ps_abc,
        nc.psum_tensor("ps_bbc", [128, TT], F32) as ps_bbc,
    ):
        sq_vals = {}
        for tb in range(NTB):
            for c in range(CH):
                gsl = c % 4
                if tb * CH + c >= 4:
                    P.wait("gpsimd", sem_ss, sq_vals[tb * CH + c - 4])
                if ("xbf", c) in free:
                    P.wait("gpsimd", *free[("xbf", c)])
                w_xbf = dma(P, "gpsimd", semG, ctrG, gsl, sl(xbf_full, c, TT),
                            xT[c * 128:(c + 1) * 128, tb * TT:(tb + 1) * TT])
                P.wait("scalar", *w_xbf)
                if ("xsq", gsl) in free:
                    P.wait("scalar", *free[("xsq", gsl)])
                P.add("scalar", lambda s, i=gsl, c=c:
                      s.activation(sl(xsq_r, i, TT), sl(xbf_full, c, TT), AF.Square)
                      .then_inc(sem_ss, 1))
                s_sq = SS.inc()
                sq_vals[tb * CH + c] = s_sq
                P.wait("tensor", *w_xbf)
                P.wait("tensor", sem_ss, s_sq)
                if tb == 1 and c == 0:
                    P.wait("tensor", sem_vs, ready[("lnfin1", 0)])
                P.add("tensor", lambda t, c=c, st=(c == 0), sp=(c == CH - 1):
                      t.matmul(ps_s1[:], ones_col[:, 0:1], sl(xbf_full, c, TT), start=st, stop=sp))
                P.add("tensor", lambda t, i=gsl, st=(c == 0), sp=(c == CH - 1):
                      t.matmul(ps_s2[:], ones_col[:, 0:1], sl(xsq_r, i, TT), start=st, stop=sp)
                      .then_inc(sem_ms, 1))
                m_chunk = MS.inc()
                free[("xsq", gsl)] = (sem_ms, m_chunk)
            ready[("lnfin1", tb)] = VS.v + 1
            rows_ready = ln_finalize(P, ps_s1, ps_s2, row_a_bf, row_b_bf, m_chunk)
            P.wait("tensor", sem_vs, rows_ready)
            if tb == 1:
                P.wait("tensor", sem_vs, ready[("applydone", 0)])
            P.add("tensor", lambda t: t.matmul(ps_abc[:], ones_row[:], row_a_bf[:],
                                               start=True, stop=True))
            P.add("tensor", lambda t: t.matmul(ps_bbc[:], ones_row[:], row_b_bf[:],
                                               start=True, stop=True).then_inc(sem_ms, 1))
            m_bc = MS.inc()
            P.wait("vector", sem_ms, m_bc)
            if tb == 1:
                P.wait("vector", sem_vs, ready[("applydone", 0)])
            P.add("vector", lambda v: v.tensor_copy(lnA_sb[:], ps_abc[:]))
            P.add("vector", lambda v: v.tensor_copy(lnB_sb[:], ps_bbc[:]).then_inc(sem_vs, 1))
            v_ab = VS.inc()
            P.wait("vector", sem_vs, v_ab)
            for c in range(CH):
                P.add("vector", lambda v, c=c:
                      v.tensor_tensor(sl(sc_bf, c % 2, TT), sl(xbf_full, c, TT), lnA_sb[:], ALU.mult)
                      .then_inc(sem_vs, 1))
                v_t = VS.inc()
                P.wait("vector", sem_vs, v_t)
                P.add("vector", lambda v, c=c, tb=tb:
                      v.tensor_tensor(xln1[:, c * TOK + tb * TT: c * TOK + (tb + 1) * TT],
                                      sl(sc_bf, c % 2, TT), lnB_sb[:], ALU.add)
                      .then_inc(sem_vs, 1))
                v_ap = VS.inc()
                free[("xbf", c)] = (sem_vs, v_ap)
            ready[("applydone", tb)] = v_ap
        with nc.Block() as b1:
            P.emit(b1)

    if upto == "b1":
        return nc, _emit_dumps(nc, dumps, locals())

    tile = Ctr()
    widx = Ctr()

    def w_block_dma(P, img, j, width=H):
        s = widx.v % 2
        if ("wslot", s) in free:
            P.wait("gpsimd", *free[("wslot", s)])
        pair = dma(P, "gpsimd", semG, ctrG, s,
                   wstream[:, s * H:s * H + width], img[j])
        return pair, s

    def proj_block(P, ps_ring, img, j, bias_col, kind, out_sb, out_off, ring_n=4):
        wv, ws_slot = w_block_dma(P, img, j)
        widx.inc()
        for tb in range(NTB):
            t = tile.v
            tile.inc()
            slot = t % ring_n
            P.wait("tensor", *wv)
            if ("ps", slot) in free:
                P.wait("tensor", *free[("ps", slot)])
            for c in range(CH):
                st, sp = (c == 0), (c == CH - 1)
                if sp:
                    P.add("tensor", lambda te, c=c, s=slot, j2=ws_slot, tb=tb:
                          te.matmul(sl(ps_ring, s, TT),
                                    wstream[:, j2 * H + c * 128: j2 * H + (c + 1) * 128],
                                    xln1[:, c * TOK + tb * TT: c * TOK + (tb + 1) * TT],
                                    start=False, stop=True).then_inc(sem_ms, 1))
                else:
                    P.add("tensor", lambda te, c=c, s=slot, j2=ws_slot, tb=tb, st=st:
                          te.matmul(sl(ps_ring, s, TT),
                                    wstream[:, j2 * H + c * 128: j2 * H + (c + 1) * 128],
                                    xln1[:, c * TOK + tb * TT: c * TOK + (tb + 1) * TT],
                                    start=st, stop=False))
            m_grp = MS.inc()
            dst = out_sb[:, out_off + tb * TT: out_off + (tb + 1) * TT]
            if kind == "phi":
                P.wait("vector", sem_ms, m_grp)
                if ("sct", t % 2) in free:
                    P.wait("vector", *free[("sct", t % 2)])
                P.add("vector", lambda v, s=slot, i=t % 2, b=bias_col:
                      v.tensor_scalar(sl(sc_t, i, TT), sl(ps_ring, s, TT), b, 0.0,
                                      ALU.add, ALU.min))
                P.add("vector", lambda v, s=slot, i=t % 2, b=bias_col:
                      v.tensor_scalar(sl(sc_r, i, TT), sl(ps_ring, s, TT), b, 0.0,
                                      ALU.add, ALU.max).then_inc(sem_vs, 1))
                v_mm = VS.inc()
                P.wait("scalar", sem_vs, v_mm)
                if ("sce", t % 2) in free:
                    P.wait("scalar", *free[("sce", t % 2)])
                P.add("scalar", lambda s, i=t % 2:
                      s.activation(sl(sc_e, i, TT), sl(sc_t, i, TT), AF.Exp).then_inc(sem_ss, 1))
                s_exp = SS.inc()
                free[("sct", t % 2)] = (sem_ss, s_exp)
                P.wait("vector", sem_ss, s_exp)
                P.wait("vector", sem_vs, v_mm)
                P.add("vector", lambda v, i=t % 2, d=dst:
                      v.tensor_tensor(d, sl(sc_e, i, TT), sl(sc_r, i, TT), ALU.add)
                      .then_inc(sem_vs, 1))
                v_done = VS.inc()
                free[("sce", t % 2)] = (sem_vs, v_done)
                free[("ps", slot)] = (sem_vs, v_done)
                ready[("tile", t)] = (sem_vs, v_done)
            else:
                P.wait("scalar", sem_ms, m_grp)
                P.add("scalar", lambda s, sp2=slot, b=bias_col, d=dst:
                      s.activation(d, sl(ps_ring, sp2, TT), AF.Identity, bias=b, scale=1.0)
                      .then_inc(sem_ss, 1))
                s_done = SS.inc()
                free[("ps", slot)] = (sem_ss, s_done)
                ready[("tile", t)] = (sem_ss, s_done)
        free[("wslot", ws_slot)] = (sem_ms, m_grp)

    P = Prog()
    with (
        nc.psum_tensor("ps_ringA", [128, 2 * TT], F32) as ps_ringA,
        nc.psum_tensor("ps_tpA0", [128, 128], BF16) as ps_tpA0,
        nc.psum_tensor("ps_tpA1", [128, 128], BF16) as ps_tpA1,
        nc.psum_tensor("ps_tpB0", [128, 128], BF16) as ps_tpB0,
        nc.psum_tensor("ps_tpB1", [128, 128], BF16) as ps_tpB1,
        nc.psum_tensor("ps_kvA", [128, 128], F32) as ps_kvA,
        nc.psum_tensor("ps_kvB", [128, 128], F32) as ps_kvB,
    ):
        k_tiles = []
        for j in range(KVH):
            proj_block(P, ps_ringA, wimg["wk"], j, b_sb["bk"][:, j:j + 1], "phi", kphi,
                       j * TOK, ring_n=2)
            k_tiles += [tile.v - 2, tile.v - 1]
        v_tiles = []
        for j in range(KVH):
            proj_block(P, ps_ringA, wimg["wv"], j, b_sb["bv"][:, j:j + 1], "id", vbf,
                       j * TOK, ring_n=2)
            v_tiles += [tile.v - 2, tile.v - 1]
        P.wait("tensor", *ready[("tile", k_tiles[-1])])
        P.wait("tensor", *ready[("tile", v_tiles[-1])])
        kv_ps = [ps_kvA, ps_kvB]
        tpA = [ps_tpA0, ps_tpA1]
        tpB = [ps_tpB0, ps_tpB1]
        mkv = {}
        tpcopy = {}

        def emit_transposes(i):
            hh, c = i // 8, i % 8
            par = i % 2
            if ("tpA", par) in free:
                P.wait("tensor", *free[("tpA", par)])
            P.add("tensor", lambda te, hh=hh, c=c, par=par:
                  te.transpose(tpA[par][:],
                               kphi[:, hh * TOK + c * 128: hh * TOK + (c + 1) * 128],
                               ident_sb[:]).then_inc(sem_ms, 1))
            m_tk = MS.inc()
            P.add("tensor", lambda te, hh=hh, c=c, par=par:
                  te.transpose(tpB[par][:],
                               vbf[:, hh * TOK + c * 128: hh * TOK + (c + 1) * 128],
                               ident_sb[:]).then_inc(sem_ms, 1))
            m_tv = MS.inc()
            P.wait("vector", sem_ms, m_tk)
            if i >= 4:
                P.wait("vector", sem_ms, mkv.get(i - 4, 0))
            P.add("vector", lambda v, i=i, par=par: v.tensor_copy(sl(kt_r, i % 4, 128), tpA[par][:]))
            P.wait("vector", sem_ms, m_tv)
            P.add("vector", lambda v, i=i, par=par: v.tensor_copy(sl(vt_r, i % 4, 128), tpB[par][:])
                  .then_inc(sem_vs, 1))
            tpcopy[i] = VS.inc()
            free[("tpA", par)] = (sem_vs, tpcopy[i])

        def emit_kv_mm(i):
            hh, c = i // 8, i % 8
            P.wait("tensor", sem_vs, tpcopy[i])
            if c == 0 and ("kvps", hh % 2) in free:
                P.wait("tensor", *free[("kvps", hh % 2)])
            st, sp = (c == 0), (c == 7)
            P.add("tensor", lambda te, hh=hh, i=i, st=st, sp=sp:
                  te.matmul(kv_ps[hh % 2][:], sl(kt_r, i % 4, 128), sl(vt_r, i % 4, 128),
                            start=st, stop=sp).then_inc(sem_ms, 1))
            mkv[i] = MS.inc()
            if sp:
                P.wait("vector", sem_ms, mkv[i])
                P.add("vector", lambda v, hh=hh:
                      v.tensor_copy(ccin_sb[:, hh * 128:(hh + 1) * 128], kv_ps[hh % 2][:])
                      .then_inc(sem_vs, 1))
                free[("kvps", hh % 2)] = (sem_vs, VS.inc())

        emit_transposes(0)
        for i in range(1, KVH * 8):
            emit_transposes(i)
            emit_kv_mm(i - 1)
        emit_kv_mm(KVH * 8 - 1)
        for hh in range(KVH):
            if hh == KVH - 1:
                P.add("vector", lambda v, hh=hh:
                      v.reduce_sum(ccin_sb[:, KVH * 128 + hh: KVH * 128 + hh + 1],
                                   kphi[:, hh * TOK:(hh + 1) * TOK], axis=AX.X)
                      .then_inc(sem_vs, 1))
            else:
                P.add("vector", lambda v, hh=hh:
                      v.reduce_sum(ccin_sb[:, KVH * 128 + hh: KVH * 128 + hh + 1],
                                   kphi[:, hh * TOK:(hh + 1) * TOK], axis=AX.X))
        v_stage = VS.inc()
        P.wait("sync", sem_vs, v_stage)
        o_stage = dma(P, "sync", semO, ctrO, 0, cc_in[:], ccin_sb[:])
        ready[("ccstage",)] = o_stage
        with nc.Block() as b2a:
            P.emit(b2a)

    if upto == "b2a":
        return nc, _emit_dumps(nc, dumps, locals())

    P = Prog()
    with nc.psum_tensor("ps_ringB", [128, 4 * TT], F32) as ps_ringB:
        for k in range(4):
            free.pop(("ps", k), None)
        P.wait("sync", sem_cs, 1)
        x_cc = dma(P, "sync", semX, ctrX, 0, ccout_sb[:], cc_out[:])
        q_tiles = []
        post = [False]
        for j in range(QH):
            proj_block(P, ps_ringB, wimg["wq"], j, b_sb["bq"][:, j:j + 1], "phi", qphi, j * TOK)
            q_tiles += [tile.v - 2, tile.v - 1]
            if j == 3:
                P.wait("gpsimd", *ready[("ccstage",)])
                P.add("gpsimd", lambda g: g.collective_compute(
                    "AllReduce", ALU.add, replica_groups=[[0, 1, 2, 3], [4, 5, 6, 7]],
                    ins=[cc_in[:]], outs=[cc_out[:]]).then_inc(sem_cs, 1))
            if j == 10 and not post[0]:
                P.wait("vector", *x_cc)
                P.add("vector", lambda v: v.tensor_copy(kvt_bf[:], ccout_sb[:, :KVH * 128]))
                P.add("vector", lambda v: v.tensor_scalar(rn_tmp[:], ccout_sb[:, KVH * 128:],
                                                          EPS_ATT, None, ALU.add)
                      .then_inc(sem_vs, 1))
                v_rt = VS.inc()
                P.wait("vector", sem_vs, v_rt)
                P.add("vector", lambda v: v.reciprocal(rn_sb[:], rn_tmp[:]).then_inc(sem_vs, 1))
                ready[("kvrdy",)] = (sem_vs, VS.inc())
                post[0] = True
        P.wait("tensor", *ready[("kvrdy",)])
        for qh in range(QH):
            kvh = qh // 4
            for tb in range(NTB):
                t = tile.v
                tile.inc()
                slot = t % 4
                P.wait("tensor", *ready[("tile", q_tiles[qh * 2 + tb])])
                if ("ps", slot) in free:
                    P.wait("tensor", *free[("ps", slot)])
                P.add("tensor", lambda te, qh=qh, kvh=kvh, tb=tb, s=slot:
                      te.matmul(sl(ps_ringB, s, TT), kvt_bf[:, kvh * 128:(kvh + 1) * 128],
                                qphi[:, qh * TOK + tb * TT: qh * TOK + (tb + 1) * TT],
                                start=True, stop=True).then_inc(sem_ms, 1))
                m_at = MS.inc()
                P.wait("scalar", *ready[("kvrdy",)])
                P.wait("scalar", sem_ms, m_at)
                P.add("scalar", lambda s, qh=qh, kvh=kvh, tb=tb, sp=slot:
                      s.activation(qphi[:, qh * TOK + tb * TT: qh * TOK + (tb + 1) * TT],
                                   sl(ps_ringB, sp, TT), AF.Identity,
                                   bias=0.0, scale=rn_sb[:, kvh:kvh + 1]).then_inc(sem_ss, 1))
                s_at = SS.inc()
                free[("ps", slot)] = (sem_ss, s_at)
                ready[("attn", qh * 2 + tb)] = (sem_ss, s_at)
        with nc.Block() as b2b:
            P.emit(b2b)

    if upto == "b2b":
        return nc, _emit_dumps(nc, dumps, locals())

    P = Prog()
    with (
        nc.psum_tensor("ps_ringC", [128, 4 * TT], F32) as ps_ringC,
        nc.psum_tensor("ps2_s1a", [1, TT], F32) as ps2_s1a,
        nc.psum_tensor("ps2_s2a", [1, TT], F32) as ps2_s2a,
        nc.psum_tensor("ps2_s1b", [1, TT], F32) as ps2_s1b,
        nc.psum_tensor("ps2_s2b", [1, TT], F32) as ps2_s2b,
    ):
        for k in range(4):
            free.pop(("ps", k), None)
        P.wait("tensor", *ready[("attn", 63)])
        st_ps = [(ps2_s1a, ps2_s2a), (ps2_s1b, ps2_s2b)]
        x2bf_rdy, x2sq_rdy, x2bf_st = {}, {}, {}
        for s in range(CH * NTB):
            j, tb = s // 2, s % 2
            wv, ws_slot = w_block_dma(P, wimg["wo"], j)
            widx.inc()
            t = tile.v
            tile.inc()
            slot = t % 4
            P.wait("tensor", *wv)
            if ("ps", slot) in free:
                P.wait("tensor", *free[("ps", slot)])
            for c in range(CH):
                st, sp2 = (c == 0), (c == CH - 1)
                if sp2:
                    P.add("tensor", lambda te, c=c, s2=slot, j2=ws_slot, tb=tb:
                          te.matmul(sl(ps_ringC, s2, TT),
                                    wstream[:, j2 * H + c * 128: j2 * H + (c + 1) * 128],
                                    qphi[:, c * TOK + tb * TT: c * TOK + (tb + 1) * TT],
                                    start=False, stop=True).then_inc(sem_ms, 1))
                else:
                    P.add("tensor", lambda te, c=c, s2=slot, j2=ws_slot, tb=tb, st=st:
                          te.matmul(sl(ps_ringC, s2, TT),
                                    wstream[:, j2 * H + c * 128: j2 * H + (c + 1) * 128],
                                    qphi[:, c * TOK + tb * TT: c * TOK + (tb + 1) * TT],
                                    start=st, stop=False))
            m_grp = MS.inc()
            free[("wslot", ws_slot)] = (sem_ms, m_grp)
            if s >= 1:
                sp_, jp, tbp = s - 1, (s - 1) // 2, (s - 1) % 2
                P.wait("tensor", sem_vs, x2bf_rdy[sp_])
                P.wait("tensor", sem_ss, x2sq_rdy[sp_])
                P.add("tensor", lambda te, tbp=tbp, i=sp_ % 2, st=(jp == 0), sp2=(jp == CH - 1):
                      te.matmul(st_ps[tbp][0][:], ones_col[:, 0:1], sl(x2bf, i, TT),
                                start=st, stop=sp2))
                P.add("tensor", lambda te, tbp=tbp, st=(jp == 0), sp2=(jp == CH - 1):
                      te.matmul(st_ps[tbp][1][:], ones_col[:, 0:1], x2sq[:],
                                start=st, stop=sp2).then_inc(sem_ms, 1))
                ready[("x2stat", sp_)] = MS.inc()
            xsl = xre_ctr.v % 4
            xre_ctr.inc()
            if ("xre", xsl) in free:
                P.wait("sync", *free[("xre", xsl)])
            x_ld = dma(P, "sync", semX, ctrX, xsl, sl(xre, xsl, TT),
                       xT[j * 128:(j + 1) * 128, tb * TT:(tb + 1) * TT])
            P.wait("scalar", sem_ms, m_grp)
            if ("sct", s % 2) in free:
                P.wait("scalar", *free[("sct", s % 2)])
            P.add("scalar", lambda sc, s2=slot, j=j, i=s % 2:
                  sc.activation(sl(sc_t, i, TT), sl(ps_ringC, s2, TT), AF.Identity,
                                bias=b_sb["bo"][:, j:j + 1], scale=1.0).then_inc(sem_ss, 1))
            s_id = SS.inc()
            free[("ps", slot)] = (sem_ss, s_id)
            P.wait("vector", sem_ss, s_id)
            P.wait("vector", *x_ld)
            if ("sco", s % 2) in free:
                P.wait("vector", *free[("sco", s % 2)])
            P.add("vector", lambda v, i=s % 2, xs=xsl:
                  v.tensor_tensor(sl(sc_o, i, TT), sl(sc_t, i, TT), sl(xre, xs, TT), ALU.add)
                  .then_inc(sem_vs, 1))
            v_x2 = VS.inc()
            free[("sct", s % 2)] = (sem_vs, v_x2)
            free[("xre", xsl)] = (sem_vs, v_x2)
            P.wait("scalar", sem_vs, v_x2)
            if s >= 1:
                P.wait("scalar", sem_ms, ready[("x2stat", s - 1)])
            P.add("scalar", lambda sc, i=s % 2:
                  sc.activation(x2sq[:], sl(sc_o, i, TT), AF.Square).then_inc(sem_ss, 1))
            x2sq_rdy[s] = SS.inc()
            if s >= 2:
                P.wait("vector", sem_ms, ready[("x2stat", s - 2)])
                P.wait("vector", *x2bf_st[s - 2])
            P.wait("vector", sem_vs, v_x2)
            P.add("vector", lambda v, i=s % 2:
                  v.tensor_copy(sl(x2bf, i, TT), sl(sc_o, i, TT)).then_inc(sem_vs, 1))
            x2bf_rdy[s] = VS.inc()
            P.wait("sync", sem_vs, x2bf_rdy[s])
            x2bf_st[s] = dma(P, "sync", semB, ctrB, s % 2,
                             x2dbf[j * 128:(j + 1) * 128, tb * TT:(tb + 1) * TT],
                             sl(x2bf, s % 2, TT))
            P.wait("sync", sem_vs, v_x2)
            o_x2 = dma(P, "sync", semO, ctrO, s % 2,
                       x2d[j * 128:(j + 1) * 128, tb * TT:(tb + 1) * TT], sl(sc_o, i := s % 2, TT))
            free[("sco", s % 2)] = o_x2
        sp_ = CH * NTB - 1
        tbp = sp_ % 2
        P.wait("tensor", sem_vs, x2bf_rdy[sp_])
        P.wait("tensor", sem_ss, x2sq_rdy[sp_])
        P.add("tensor", lambda te, tbp=tbp, i=sp_ % 2:
              te.matmul(st_ps[tbp][0][:], ones_col[:, 0:1], sl(x2bf, i, TT),
                        start=False, stop=True))
        P.add("tensor", lambda te, tbp=tbp:
              te.matmul(st_ps[tbp][1][:], ones_col[:, 0:1], x2sq[:],
                        start=False, stop=True).then_inc(sem_ms, 1))
        m_last = MS.inc()

        def ln2_fin(P, tb, stats_ms):
            inv = 1.0 / H
            P.wait("vector", sem_ms, stats_ms)
            P.add("vector", lambda v, tb=tb:
                  v.tensor_scalar(row_mean[:], st_ps[tb][0][:], inv, None, ALU.mult))
            P.add("vector", lambda v, tb=tb:
                  v.tensor_scalar(row_var[:], st_ps[tb][1][:], inv, None, ALU.mult)
                  .then_inc(sem_vs, 1))
            v1 = VS.inc()
            P.wait("vector", sem_vs, v1)
            P.add("vector", lambda v: v.tensor_tensor(row_r[:], row_mean[:], row_mean[:], ALU.mult)
                  .then_inc(sem_vs, 1))
            v2 = VS.inc()
            P.wait("vector", sem_vs, v2)
            P.add("vector", lambda v: v.tensor_tensor(row_var[:], row_var[:], row_r[:], ALU.subtract)
                  .then_inc(sem_vs, 1))
            v_var = VS.inc()
            P.wait("scalar", sem_vs, v_var)
            P.add("scalar", lambda sc: sc.activation(row_b[:], row_var[:], AF.Ln,
                                                     bias=np.float32(EPS_LN), scale=1.0)
                  .then_inc(sem_ss, 1))
            s_ln = SS.inc()
            P.wait("scalar", sem_ss, s_ln)
            P.add("scalar", lambda sc: sc.activation(row_r[:], row_b[:], AF.Exp, bias=0.0, scale=-0.5)
                  .then_inc(sem_ss, 1))
            s_r = SS.inc()
            P.wait("vector", sem_ss, s_r)
            P.add("vector", lambda v: v.scalar_tensor_tensor(row_b[:], row_mean[:], -1.0, row_r[:],
                                                             ALU.mult, ALU.mult).then_inc(sem_vs, 1))
            v4 = VS.inc()
            P.add("vector", lambda v, tb=tb: v.tensor_copy(ln2a[tb][:], row_r[:]))
            P.wait("vector", sem_vs, v4)
            P.add("vector", lambda v, tb=tb: v.tensor_copy(ln2b[tb][:], row_b[:]).then_inc(sem_vs, 1))
            return VS.inc()

        ln2_fin(P, 0, ready[("x2stat", CH * NTB - 2)])
        ln2_fin(P, 1, m_last)
        P.add("sync", lambda sy, v0=ctrO[0].v, v1=ctrO[1].v, b0=ctrB[0].v, b1=ctrB[1].v:
              (sy.wait_ge(semO[0], v0), sy.wait_ge(semO[1], v1),
               sy.wait_ge(semB[0], b0), sy.wait_ge(semB[1], b1)))
        with nc.Block() as b3:
            P.emit(b3)

    if upto == "b3":
        return nc, _emit_dumps(nc, dumps, locals())

    P = Prog()
    with (
        nc.psum_tensor("ps_g", [128, 2 * TOK], F32) as ps_g,
        nc.psum_tensor("ps_u", [128, 2 * TOK], F32) as ps_u,
    ):
        for k in range(4):
            free.pop(("ps", k), None)
        free.pop(("wslot", 0), None)
        free.pop(("wslot", 1), None)

        for tb in range(NTB):
            P.add("tensor", lambda te, tb=tb:
                  te.matmul(ps_g[:, tb * TT:(tb + 1) * TT], ones_row[:], ln2a[tb][:],
                            start=True, stop=True))
            P.add("tensor", lambda te, tb=tb:
                  te.matmul(ps_g[:, TOK + tb * TT: TOK + (tb + 1) * TT], ones_row[:],
                            ln2b[tb][:], start=True, stop=True).then_inc(sem_ms, 1))
            MS.inc()
        P.wait("vector", sem_ms, MS.v)
        P.add("vector", lambda v: v.tensor_copy(lnA2[:], ps_g[:, 0:TOK]))
        P.add("vector", lambda v: v.tensor_copy(lnB2[:], ps_g[:, TOK:2 * TOK])
              .then_inc(sem_vs, 1))
        v_ab = VS.inc()
        P.wait("vector", sem_vs, v_ab)

        apply_v = {}
        for c in range(CH):
            asl = c % 2
            if ("x2a", asl) in free:
                P.wait("sync", *free[("x2a", asl)])
            x_ld = dma(P, "sync", semA, ctrA, asl, x2a[:, asl * TOK:(asl + 1) * TOK],
                       x2dbf[c * 128:(c + 1) * 128, :])
            P.wait("vector", *x_ld)
            P.add("vector", lambda v, i=asl, c=c:
                  v.tensor_tensor(sl(scbf2, c % 2, TOK), x2a[:, i * TOK:(i + 1) * TOK],
                                  lnA2[:], ALU.mult).then_inc(sem_vs, 1))
            v_t = VS.inc()
            P.wait("vector", sem_vs, v_t)
            P.add("vector", lambda v, c=c:
                  v.tensor_tensor(xln2[:, c * TOK:(c + 1) * TOK], sl(scbf2, c % 2, TOK),
                                  lnB2[:], ALU.add).then_inc(sem_vs, 1))
            v_ap = VS.inc()
            apply_v[c] = v_ap
            free[("x2a", asl)] = (sem_vs, v_ap)

        h_st = {}

        def gu_load(jt, st_i):
            gslot = st_i * 2 + (jt % 2)
            if ("gu", gslot) in free:
                P.wait("gpsimd", *free[("gu", gslot)])
            return dma(P, "gpsimd", semG, ctrG, gslot, sl(gu, gslot, H),
                       (wimg["wg"] if st_i == 0 else wimg["wu"])[jt])

        pre = {}
        for jt0 in range(2):
            for st_i in (0, 1):
                pre[(jt0, st_i)] = gu_load(jt0, st_i)
        for jt in range(IT):
            guw = {}
            for st_i in (0, 1):
                guw[st_i] = pre[(jt, st_i)] if jt < 2 else gu_load(jt, st_i)
            pslot = jt % 2
            P.wait("tensor", *guw[0])
            if ("psG", pslot) in free:
                P.wait("tensor", *free[("psG", pslot)])
            gs = 0 * 2 + jt % 2
            for half in range(NTB):
                for c in range(CH):
                    if jt == 0 and half == 0:
                        P.wait("tensor", sem_vs, apply_v[c])
                    lastg = (half == NTB - 1 and c == CH - 1)
                    P.add("tensor", lambda te, c=c, h2=half, ps=pslot, gs=gs,
                          st=(c == 0), sp=(c == CH - 1), l=lastg:
                          te.matmul(ps_g[:, ps * TOK + h2 * TT: ps * TOK + h2 * TT + TT],
                                    gu[:, gs * H + c * 128: gs * H + (c + 1) * 128],
                                    xln2[:, c * TOK + h2 * TT: c * TOK + h2 * TT + TT],
                                    start=st, stop=sp)
                          .then_inc(sem_ms, 1) if l else
                          te.matmul(ps_g[:, ps * TOK + h2 * TT: ps * TOK + h2 * TT + TT],
                                    gu[:, gs * H + c * 128: gs * H + (c + 1) * 128],
                                    xln2[:, c * TOK + h2 * TT: c * TOK + h2 * TT + TT],
                                    start=st, stop=sp))
            m_gate = MS.inc()
            free[("gu", 0 * 2 + jt % 2)] = (sem_ms, m_gate)
            P.wait("tensor", *guw[1])
            if ("psU", pslot) in free:
                P.wait("tensor", *free[("psU", pslot)])
            gs = 1 * 2 + jt % 2
            for half in range(NTB):
                for c in range(CH):
                    lastu = (half == NTB - 1 and c == CH - 1)
                    P.add("tensor", lambda te, c=c, h2=half, ps=pslot, gs=gs,
                          st=(c == 0), sp=(c == CH - 1), l=lastu:
                          te.matmul(ps_u[:, ps * TOK + h2 * TT: ps * TOK + h2 * TT + TT],
                                    gu[:, gs * H + c * 128: gs * H + (c + 1) * 128],
                                    xln2[:, c * TOK + h2 * TT: c * TOK + h2 * TT + TT],
                                    start=st, stop=sp)
                          .then_inc(sem_ms, 1) if l else
                          te.matmul(ps_u[:, ps * TOK + h2 * TT: ps * TOK + h2 * TT + TT],
                                    gu[:, gs * H + c * 128: gs * H + (c + 1) * 128],
                                    xln2[:, c * TOK + h2 * TT: c * TOK + h2 * TT + TT],
                                    start=st, stop=sp))
            m_up = MS.inc()
            free[("gu", 1 * 2 + jt % 2)] = (sem_ms, m_up)
            P.wait("scalar", sem_ms, m_gate)
            if ("scr2", jt % 2) in free:
                P.wait("scalar", *free[("scr2", jt % 2)])
            P.add("scalar", lambda s, ps=pslot, jt=jt, i=jt % 2:
                  s.activation(sl(scr2, i, TOK), ps_g[:, ps * TOK:(ps + 1) * TOK], AF.Silu,
                               bias=0.0, scale=b_sb["ag"][:, jt:jt + 1]).then_inc(sem_ss, 1))
            s_silu = SS.inc()
            free[("psG", pslot)] = (sem_ss, s_silu)
            P.wait("vector", sem_ms, m_up)
            if ("sce2", jt % 2) in free:
                P.wait("vector", *free[("sce2", jt % 2)])
            P.add("vector", lambda v, ps=pslot, jt=jt, i=jt % 2:
                  v.tensor_scalar(sl(sce2, i, TOK), ps_u[:, ps * TOK:(ps + 1) * TOK],
                                  b_sb["au"][:, jt:jt + 1], None, ALU.mult)
                  .then_inc(sem_vs, 1))
            v_u = VS.inc()
            free[("psU", pslot)] = (sem_vs, v_u)
            hsl = jt % 4
            P.wait("vector", sem_ss, s_silu)
            P.wait("vector", sem_vs, v_u)
            if ("hr", hsl) in free:
                P.wait("vector", *free[("hr", hsl)])
            P.add("vector", lambda v, jt=jt, i=jt % 2, hsl=hsl:
                  v.tensor_tensor(sl(hr, hsl, TOK), sl(scr2, i, TOK), sl(sce2, i, TOK),
                                  ALU.mult).then_inc(sem_vs, 1))
            v_h = VS.inc()
            free[("scr2", jt % 2)] = (sem_vs, v_h)
            free[("sce2", jt % 2)] = (sem_vs, v_h)
            P.wait("sync", sem_vs, v_h)
            h_st[jt] = dma(P, "sync", semW, ctrW, hsl,
                           hd[jt * 128:(jt + 1) * 128, :], sl(hr, hsl, TOK))
            free[("hr", hsl)] = h_st[jt]

        for k in range(4):
            if ("xre", k) in free:
                P.wait("sync", *free[("xre", k)])
                free.pop(("xre", k))
        for k in range(2):
            P.wait("tensor", *free[("psG", k)])
            P.wait("tensor", *free[("psU", k)])
            free.pop(("psG", k))
            free.pop(("psU", k))
        dn_ps = [ps_g, ps_g, ps_u, ps_u]
        NGT = (CH // 4) * IT
        cast_done = {}

        def dn_wd_dma(gt):
            ob, jt = gt // IT, gt % IT
            slot = gt % 4
            if ("stag", slot) in free:
                P.wait("scalar", *free[("stag", slot)])
            return dma(P, "scalar", semD, ctrD, slot, sl(wdf, slot, 512),
                       wimg["wdob"][ob][:, jt * 512:(jt + 1) * 512])

        def dn_wd_cast(gt, wv):
            slot = gt % 4
            P.wait("scalar", *wv)
            if ("wdsb", slot) in free:
                P.wait("scalar", *free[("wdsb", slot)])
            P.add("scalar", lambda sc, slot=slot:
                  sc.activation(sl(wds, slot, 512), sl(wdf, slot, 512), AF.Identity,
                                bias=0.0, scale=1.0).then_inc(sem_ss, 1))
            cast_done[gt] = SS.inc()
            free[("stag", slot)] = (sem_ss, cast_done[gt])

        wdv = {gt: dn_wd_dma(gt) for gt in range(2)}
        for gt in range(NGT):
            ob, jt = gt // IT, gt % IT
            slot = gt % 4
            if gt + 2 < NGT:
                wdv[gt + 2] = dn_wd_dma(gt + 2)
            dn_wd_cast(gt, wdv.pop(gt))
            hsl = gt % 4
            if ob == 0 and jt < 2 and ("x2a", hsl) in free:
                P.wait("sync", *free[("x2a", hsl)])
            if ("hr", hsl) in free:
                P.wait("sync", *free[("hr", hsl)])
            P.wait("sync", *h_st[jt])
            hv = dma(P, "sync", semA, ctrA, hsl, sl(hr, hsl, TOK),
                     hd[jt * 128:(jt + 1) * 128, :])
            P.wait("tensor", sem_ss, cast_done[gt])
            P.wait("tensor", *hv)
            for oi in range(4):
                if jt == 0 and ("dnps", oi) in free:
                    P.wait("tensor", *free[("dnps", oi)])
                for half in range(2):
                    lastm = (oi == 3 and half == 1)
                    P.add("tensor", lambda te, oi=oi, h2=half, wsl=slot, hsl=hsl,
                          st=(jt == 0), sp=(jt == IT - 1), l=lastm:
                          te.matmul(dn_ps[oi][:, (oi % 2) * TOK + h2 * TT:
                                              (oi % 2) * TOK + h2 * TT + TT],
                                    wds[:, wsl * 512 + oi * 128: wsl * 512 + (oi + 1) * 128],
                                    hr[:, hsl * TOK + h2 * TT: hsl * TOK + h2 * TT + TT],
                                    start=st, stop=sp)
                          .then_inc(sem_ms, 1) if l else
                          te.matmul(dn_ps[oi][:, (oi % 2) * TOK + h2 * TT:
                                              (oi % 2) * TOK + h2 * TT + TT],
                                    wds[:, wsl * 512 + oi * 128: wsl * 512 + (oi + 1) * 128],
                                    hr[:, hsl * TOK + h2 * TT: hsl * TOK + h2 * TT + TT],
                                    start=st, stop=sp))
            m_gt = MS.inc()
            free[("wdsb", slot)] = (sem_ms, m_gt)
            free[("hr", hsl)] = (sem_ms, m_gt)
            if jt != IT - 1:
                continue
            m_dn = m_gt
            for oi in range(4):
                o = ob * 4 + oi
                esl = o % 2
                P.wait("vector", sem_ms, m_dn)
                if ("dnt", esl) in free:
                    P.wait("vector", *free[("dnt", esl)])
                P.add("vector", lambda v, oi=oi, o=o, i=esl:
                      v.tensor_scalar(sl(dn_t, i, TOK), dn_ps[oi][:, (oi % 2) * TOK:
                                                                  (oi % 2 + 1) * TOK],
                                      b_sb["ad"][:, o:o + 1], None, ALU.mult)
                      .then_inc(sem_vs, 1))
                v_scale = VS.inc()
                free[("dnps", oi)] = (sem_vs, v_scale)
                xsl = o % 2
                if ("xre2", xsl) in free:
                    P.wait("sync", *free[("xre2", xsl)])
                x_ld = dma(P, "sync", semX, ctrX, xsl, xre[:, xsl * TOK:(xsl + 1) * TOK],
                           x2d[o * 128:(o + 1) * 128, :])
                P.wait("vector", sem_vs, v_scale)
                P.wait("vector", *x_ld)
                if ("dno", esl) in free:
                    P.wait("vector", *free[("dno", esl)])
                P.add("vector", lambda v, i=esl, xs=xsl:
                      v.tensor_tensor(sl(dn_o, i, TOK), sl(dn_t, i, TOK),
                                      xre[:, xs * TOK:(xs + 1) * TOK], ALU.add)
                      .then_inc(sem_vs, 1))
                v_out = VS.inc()
                free[("dnt", esl)] = (sem_vs, v_out)
                free[("xre2", xsl)] = (sem_vs, v_out)
                P.wait("sync", sem_vs, v_out)
                o_out = dma(P, "sync", semO, ctrO, o % 2,
                            outT[o * 128:(o + 1) * 128, :], sl(dn_o, esl, TOK))
                free[("dno", esl)] = o_out
        P.add("sync", lambda sy, v0=ctrO[0].v, v1=ctrO[1].v:
              (sy.wait_ge(semO[0], v0), sy.wait_ge(semO[1], v1)))
        with nc.Block() as b4:
            P.emit(b4)

    return nc, _emit_dumps(nc, dumps, locals())


def _emit_dumps(nc, dumps, env):
    if not dumps:
        return []
    names = []
    with nc.semaphore("dmp") as dsem, nc.Block() as bd:
        def _(sy):
            n = 0
            for nm in dumps:
                src = env[nm]
                ap = src if isinstance(src, bass.AP) else src.ap() if hasattr(src, "ap") else src[:]
                shape = list(ap.tensor.shape) if hasattr(ap, "tensor") else list(src.shape)
                dt = ap.dtype
                out = nc.dram_tensor("dump_" + nm, shape, dt, kind="ExternalOutput").ap()
                sy.dma_start(out[:], ap).then_inc(dsem, 16)
                n += 16
                names.append("dump_" + nm)
            sy.wait_ge(dsem, n)
        bd.sync(_)
    return names


_NC_CACHE = {}


def _prep_inputs(inputs):
    x = np.asarray(inputs["L0_attn_hidden_states"], dtype=np.float32)
    xf = np.ascontiguousarray(x.reshape(2 * 4096, H))

    def img(w, nblk, wide):
        w = np.asarray(w, dtype=np.float32)
        return np.ascontiguousarray(
            w.reshape(nblk, 128, wide // 128, 128).transpose(0, 3, 2, 1).reshape(nblk, 128, wide))

    def cols(v, n):
        return np.ascontiguousarray(np.asarray(v, dtype=np.float32).reshape(n, 128).T)

    common = {
        "wq": img(inputs["L0_attn_Wq"], QH, H),
        "wk": img(inputs["L0_attn_Wk"], KVH, H),
        "wv": img(inputs["L0_attn_Wv"], KVH, H),
        "wo": img(inputs["L0_attn_Wo"], CH, H),
        "wg": img(inputs["L0_ffn_Wg"], IT, H),
        "wu": img(inputs["L0_ffn_Wu"], IT, H),
        "wdob": img(inputs["L0_ffn_Wd"], CH, IT * 128)
                .reshape(CH // 4, 4, 128, IT, 128).transpose(0, 2, 3, 1, 4)
                .reshape(CH // 4, 128, IT * 512).copy(),
        "bq": cols(inputs["L0_attn_bq"], QH),
        "bk": cols(inputs["L0_attn_bk"], KVH),
        "bv": cols(inputs["L0_attn_bv"], KVH),
        "bo": cols(inputs["L0_attn_bo"], CH),
        "ag": cols(inputs["L0_ffn_ag"], IT),
        "au": cols(inputs["L0_ffn_au"], IT),
        "ad": cols(inputs["L0_ffn_ad"], CH),
        "ident": np.eye(128, dtype=ml_dtypes.bfloat16),
    }
    in_maps = []
    for c in range(8):
        m = dict(common)
        m["xT"] = np.ascontiguousarray(xf[c * TOK:(c + 1) * TOK].T)
        in_maps.append(m)
    return in_maps


def kernel(**inputs):
    in_maps = _prep_inputs(inputs)
    if "nc" not in _NC_CACHE:
        _NC_CACHE["nc"] = build()[0]
    nc = _NC_CACHE["nc"]
    res = run_bass_kernel_spmd(nc, in_maps, core_ids=list(range(8)))
    parts = [res.results[c]["outT"].T for c in range(8)]
    out = np.concatenate(parts, axis=0).reshape(2, 4096, H)
    return np.ascontiguousarray(out, dtype=np.float32)


# version 08 (final)
# speedup vs baseline: 1.0774x; 1.0774x over previous
import numpy as np
import ml_dtypes
import concourse.bass as bass
import concourse.mybir as mybir
from concourse.bass_utils import run_bass_kernel_spmd

F32 = mybir.dt.float32
BF16 = mybir.dt.bfloat16
AF = mybir.ActivationFunctionType
ALU = mybir.AluOpType
AX = mybir.AxisListType

H = 4096
CH = 32
TOK = 1024
TT = 512
NTB = 2
KVH = 8
QH = 32
IT = 86
EPS_LN = 1.0e-5
EPS_ATT = 1.0e-10
CCW = KVH * 128 + KVH

ENGS = ("sync", "gpsimd", "tensor", "scalar", "vector")


class Prog:
    def __init__(self):
        self.q = {e: [] for e in ENGS}
        self.hi = {e: {} for e in ENGS}

    def add(self, eng, fn):
        self.q[eng].append(fn)

    def wait(self, eng, sem, val):
        if val <= 0:
            return
        key = id(sem)
        if self.hi[eng].get(key, -1) >= val:
            return
        self.hi[eng][key] = val
        self.q[eng].append(lambda e, s=sem, v=val: e.wait_ge(s, v))

    def emit(self, block):
        for name in ENGS:
            ops = self.q[name]
            if not ops:
                continue
            getattr(block, name)(lambda e, ops=ops: [f(e) for f in ops])


class Ctr:
    def __init__(self):
        self.v = 0

    def inc(self, n=1):
        self.v += n
        return self.v


def build(upto="b4", dumps=()):
    nc = bass.Bass("TRN2", target_bir_lowering=False, debug=False)

    xT = nc.dram_tensor("xT", [H, TOK], F32, kind="ExternalInput").ap()
    wimg = {}
    for nm, nblk, wide in (("wq", QH, H), ("wk", KVH, H), ("wv", KVH, H),
                           ("wo", CH, H), ("wg", IT, H), ("wu", IT, H),
                           ("wdob", CH // 4, IT * 512)):
        wimg[nm] = nc.dram_tensor(nm, [nblk, 128, wide], F32, kind="ExternalInput").ap()
    bvecs = {}
    for nm, w in (("bq", QH), ("bk", KVH), ("bv", KVH), ("bo", CH),
                  ("ag", IT), ("au", IT), ("ad", CH)):
        bvecs[nm] = nc.dram_tensor(nm, [128, w], F32, kind="ExternalInput").ap()
    ident_in = nc.dram_tensor("ident", [128, 128], BF16, kind="ExternalInput").ap()
    outT = nc.dram_tensor("outT", [H, TOK], F32, kind="ExternalOutput").ap()
    x2d = nc.dram_tensor("x2d", [H, TOK], F32, kind="Internal").ap()
    x2dbf = nc.dram_tensor("x2dbf", [H, TOK], BF16, kind="Internal").ap()
    hd = nc.dram_tensor("hd", [IT * 128, TOK], BF16, kind="Internal").ap()
    cc_in = nc.dram_tensor("cc_in", [128, CCW], F32, kind="Internal").ap()
    cc_out = nc.dram_tensor("cc_out", [128, CCW], F32, kind="Internal").ap()

    sem_ms = nc.alloc_semaphore("s_ms")
    sem_vs = nc.alloc_semaphore("s_vs")
    sem_ss = nc.alloc_semaphore("s_ss")
    sem_cs = nc.alloc_semaphore("s_cs")
    MS, VS, SS = Ctr(), Ctr(), Ctr()
    semG = [nc.alloc_semaphore(f"s_g{i}") for i in range(4)]
    ctrG = [Ctr() for _ in range(4)]
    semX = [nc.alloc_semaphore(f"s_x{i}") for i in range(4)]
    ctrX = [Ctr() for _ in range(4)]
    semO = [nc.alloc_semaphore(f"s_o{i}") for i in range(2)]
    ctrO = [Ctr() for _ in range(2)]
    semA = [nc.alloc_semaphore(f"s_a{i}") for i in range(4)]
    ctrA = [Ctr() for _ in range(4)]
    semB = [nc.alloc_semaphore(f"s_b{i}") for i in range(2)]
    ctrB = [Ctr() for _ in range(2)]
    semW = [nc.alloc_semaphore(f"s_w{i}") for i in range(4)]
    ctrW = [Ctr() for _ in range(4)]
    semD = [nc.alloc_semaphore(f"s_d{i}") for i in range(4)]
    ctrD = [Ctr() for _ in range(4)]
    semE = [nc.alloc_semaphore(f"s_e{i}") for i in range(3)]
    ctrE = [Ctr() for _ in range(3)]

    ident_sb = nc.alloc_sbuf_tensor("ident_sb", [128, 128], BF16)
    ones_col = nc.alloc_sbuf_tensor("ones_col", [128, 16], BF16)
    ones_row = nc.alloc_sbuf_tensor("ones_row", [1, 128], BF16)
    b_sb = {nm: nc.alloc_sbuf_tensor(nm + "_sb", [128, w], F32)
            for nm, w in (("bq", QH), ("bk", KVH), ("bv", KVH), ("bo", CH),
                          ("ag", IT), ("au", IT), ("ad", CH))}
    ln2a = [nc.alloc_sbuf_tensor(f"ln2a{t}", [1, TT], BF16) for t in range(NTB)]
    ln2b = [nc.alloc_sbuf_tensor(f"ln2b{t}", [1, TT], BF16) for t in range(NTB)]
    sc_t = nc.alloc_sbuf_tensor("sc_t", [128, 2 * TT], F32)
    sc_e = nc.alloc_sbuf_tensor("sc_e", [128, 2 * TT], BF16)
    sc_r = nc.alloc_sbuf_tensor("sc_r", [128, 2 * TT], BF16)
    sc_bf = nc.alloc_sbuf_tensor("sc_bf", [128, 2 * TT], BF16)
    lnA_sb = nc.alloc_sbuf_tensor("lnA_sb", [128, TT], BF16)
    lnB_sb = nc.alloc_sbuf_tensor("lnB_sb", [128, TT], BF16)
    sc_o = nc.alloc_sbuf_tensor("sc_o", [128, 2 * TT], F32)
    xre = nc.alloc_sbuf_tensor("xre", [128, 4 * TT], F32)
    wstream = nc.alloc_sbuf_tensor("wstream", [128, IT * 128], BF16)
    kvt_bf = nc.alloc_sbuf_tensor("kvt_bf", [128, KVH * 128], BF16)
    rn_sb = nc.alloc_sbuf_tensor("rn_sb", [128, KVH], F32)
    rn_tmp = nc.alloc_sbuf_tensor("rn_tmp", [128, KVH], F32)
    x2a = nc.alloc_sbuf_tensor("x2a", [128, 4 * TT], BF16)

    TOP = 229344
    MB_BYTES = 88064
    MA_BYTES = 65536
    MB0 = (TOP - MB_BYTES) // 32 * 32
    MA0 = (MB0 - MA_BYTES) // 32 * 32
    A = lambda name, shape, dt, off: nc.alloc_sbuf_tensor_at(name, shape, dt, offset=off)
    xln1 = A("xln1", [128, CH * TOK], BF16, MA0)
    xln2 = A("xln2", [128, CH * TOK], BF16, MA0)
    gu = A("gu", [128, 4 * H], BF16, MB0)
    wdf = A("wdf", [128, 4 * 512], F32, MB0)
    stg0 = A("stg0", [128, 2048], F32, MB0 + 45056)
    stg1 = A("stg1", [128, 2048], F32, MB0 + 53248)
    stg2 = A("stg2", [128, 2048], F32, MB0 + 61440)
    hr = A("hr", [128, 4 * TOK], BF16, MB0 + 32768)
    wds = A("wds", [128, 4 * 512], BF16, MB0 + 40960)
    dn_t = A("dn_t", [128, 2 * TOK], F32, MB0 + 45056)
    dn_o = A("dn_o", [128, 2 * TOK], F32, MB0 + 53248)
    lnA2 = A("lnA2", [128, TOK], BF16, MB0 + 61440)
    lnB2 = A("lnB2", [128, TOK], BF16, MB0 + 63488)
    scbf2 = A("scbf2", [128, 2 * TOK], BF16, MB0 + 65536)
    scr2 = A("scr2", [128, 2 * TOK], BF16, MB0 + 69632)
    sce2 = A("sce2", [128, 2 * TOK], BF16, MB0 + 73728)
    xbf_full = A("xbf_full", [128, CH * TT], BF16, MB0)
    xsq_r = A("xsq_r", [128, 4 * TT], BF16, MB0 + 32768)
    kphi = A("kphi", [128, KVH * TOK], BF16, MB0)
    vbf = A("vbf", [128, KVH * TOK], BF16, MB0 + 16384)
    kt_r = A("kt_r", [128, 4 * 128], BF16, MB0 + 32768)
    vt_r = A("vt_r", [128, 4 * 128], BF16, MB0 + 33792)
    qphi = A("qphi", [128, QH * TOK], BF16, MB0)
    row_mean = A("row_mean", [1, TT], F32, MB0 + 65536)
    row_var = A("row_var", [1, TT], F32, MB0 + 67584)
    row_r = A("row_r", [1, TT], F32, MB0 + 69632)
    row_a_bf = A("row_a_bf", [1, TT], BF16, MB0 + 71680)
    row_b_bf = A("row_b_bf", [1, TT], BF16, MB0 + 72704)
    ccin_sb = A("ccin_sb", [128, CCW], F32, MB0 + 73728)
    ccout_sb = A("ccout_sb", [128, CCW], F32, MB0 + 77888)
    x2bf = A("x2bf", [128, 2 * TT], BF16, MB0 + 82048)
    x2sq = A("x2sq", [128, TT], BF16, MB0 + 84096)
    row_b = A("row_b", [1, TT], F32, MB0 + 85120)

    assert nc.sbuf_base <= MA0, (nc.sbuf_base, MA0)

    def sl(t, i, w):
        return t[:, i * w:(i + 1) * w]

    free = {}
    ready = {}
    xre_ctr = Ctr()

    def dma(P, eng, sems, ctrs, idx, dst, src):
        P.add(eng, lambda e, d=dst, s=src, sm=sems[idx]: e.dma_start(d, s).then_inc(sm, 16))
        return (sems[idx], ctrs[idx].inc(16))

    P = Prog()
    _b0_target = ctrX[0].v + 8 * 16
    def b0_sync(sy, v_target=_b0_target):
        sy.dma_start(ident_sb[:], ident_in[:]).then_inc(semX[0], 16)
        for nm in ("bq", "bk", "bv", "bo", "ag", "au", "ad"):
            sy.dma_start(b_sb[nm][:], bvecs[nm][:]).then_inc(semX[0], 16)
        sy.wait_ge(semX[0], v_target)
    P.add("sync", b0_sync)
    ctrX[0].inc(8 * 16)
    P.add("vector", lambda v: v.memset(ones_col[:], 1.0))
    P.add("vector", lambda v: v.memset(ones_row[:], 1.0))
    with nc.Block() as b0:
        P.emit(b0)

    def ln_finalize(P, s1_ps, s2_ps, a_dst, b_dst, stats_done_ms):
        inv = 1.0 / H
        P.wait("vector", sem_ms, stats_done_ms)
        P.add("vector", lambda v: v.tensor_scalar(row_mean[:], s1_ps[:], inv, None, ALU.mult))
        P.add("vector", lambda v: v.tensor_scalar(row_var[:], s2_ps[:], inv, None, ALU.mult)
              .then_inc(sem_vs, 1))
        v1 = VS.inc()
        P.wait("vector", sem_vs, v1)
        P.add("vector", lambda v: v.tensor_tensor(row_r[:], row_mean[:], row_mean[:], ALU.mult)
              .then_inc(sem_vs, 1))
        v2 = VS.inc()
        P.wait("vector", sem_vs, v2)
        P.add("vector", lambda v: v.tensor_tensor(row_var[:], row_var[:], row_r[:], ALU.subtract)
              .then_inc(sem_vs, 1))
        v_var = VS.inc()
        P.wait("scalar", sem_vs, v_var)
        P.add("scalar", lambda s: s.activation(row_b[:], row_var[:], AF.Ln,
                                               bias=np.float32(EPS_LN), scale=1.0)
              .then_inc(sem_ss, 1))
        s_ln = SS.inc()
        P.wait("scalar", sem_ss, s_ln)
        P.add("scalar", lambda s: s.activation(row_r[:], row_b[:], AF.Exp, bias=0.0, scale=-0.5)
              .then_inc(sem_ss, 1))
        s_r = SS.inc()
        P.wait("vector", sem_ss, s_r)
        P.add("vector", lambda v: v.scalar_tensor_tensor(row_b[:], row_mean[:], -1.0, row_r[:],
                                                         ALU.mult, ALU.mult).then_inc(sem_vs, 1))
        v4 = VS.inc()
        P.add("vector", lambda v, a=a_dst: v.tensor_copy(a[:], row_r[:]))
        P.wait("vector", sem_vs, v4)
        P.add("vector", lambda v, b=b_dst: v.tensor_copy(b[:], row_b[:]).then_inc(sem_vs, 1))
        return VS.inc()

    P = Prog()
    with (
        nc.psum_tensor("ps_s1", [1, TT], F32) as ps_s1,
        nc.psum_tensor("ps_s2", [1, TT], F32) as ps_s2,
        nc.psum_tensor("ps_abc", [128, TT], F32) as ps_abc,
        nc.psum_tensor("ps_bbc", [128, TT], F32) as ps_bbc,
    ):
        sq_vals = {}
        for tb in range(NTB):
            for c in range(CH):
                gsl = c % 4
                if tb * CH + c >= 4:
                    P.wait("gpsimd", sem_ss, sq_vals[tb * CH + c - 4])
                if ("xbf", c) in free:
                    P.wait("gpsimd", *free[("xbf", c)])
                w_xbf = dma(P, "gpsimd", semG, ctrG, gsl, sl(xbf_full, c, TT),
                            xT[c * 128:(c + 1) * 128, tb * TT:(tb + 1) * TT])
                P.wait("scalar", *w_xbf)
                if ("xsq", gsl) in free:
                    P.wait("scalar", *free[("xsq", gsl)])
                P.add("scalar", lambda s, i=gsl, c=c:
                      s.activation(sl(xsq_r, i, TT), sl(xbf_full, c, TT), AF.Square)
                      .then_inc(sem_ss, 1))
                s_sq = SS.inc()
                sq_vals[tb * CH + c] = s_sq
                P.wait("tensor", *w_xbf)
                P.wait("tensor", sem_ss, s_sq)
                if tb == 1 and c == 0:
                    P.wait("tensor", sem_vs, ready[("lnfin1", 0)])
                P.add("tensor", lambda t, c=c, st=(c == 0), sp=(c == CH - 1):
                      t.matmul(ps_s1[:], ones_col[:, 0:1], sl(xbf_full, c, TT), start=st, stop=sp))
                P.add("tensor", lambda t, i=gsl, st=(c == 0), sp=(c == CH - 1):
                      t.matmul(ps_s2[:], ones_col[:, 0:1], sl(xsq_r, i, TT), start=st, stop=sp)
                      .then_inc(sem_ms, 1))
                m_chunk = MS.inc()
                free[("xsq", gsl)] = (sem_ms, m_chunk)
            ready[("lnfin1", tb)] = VS.v + 1
            rows_ready = ln_finalize(P, ps_s1, ps_s2, row_a_bf, row_b_bf, m_chunk)
            P.wait("tensor", sem_vs, rows_ready)
            if tb == 1:
                P.wait("tensor", sem_vs, ready[("applydone", 0)])
            P.add("tensor", lambda t: t.matmul(ps_abc[:], ones_row[:], row_a_bf[:],
                                               start=True, stop=True))
            P.add("tensor", lambda t: t.matmul(ps_bbc[:], ones_row[:], row_b_bf[:],
                                               start=True, stop=True).then_inc(sem_ms, 1))
            m_bc = MS.inc()
            P.wait("vector", sem_ms, m_bc)
            if tb == 1:
                P.wait("vector", sem_vs, ready[("applydone", 0)])
            P.add("vector", lambda v: v.tensor_copy(lnA_sb[:], ps_abc[:]))
            P.add("vector", lambda v: v.tensor_copy(lnB_sb[:], ps_bbc[:]).then_inc(sem_vs, 1))
            v_ab = VS.inc()
            P.wait("vector", sem_vs, v_ab)
            for c in range(CH):
                P.add("vector", lambda v, c=c:
                      v.tensor_tensor(sl(sc_bf, c % 2, TT), sl(xbf_full, c, TT), lnA_sb[:], ALU.mult)
                      .then_inc(sem_vs, 1))
                v_t = VS.inc()
                P.wait("vector", sem_vs, v_t)
                P.add("vector", lambda v, c=c, tb=tb:
                      v.tensor_tensor(xln1[:, c * TOK + tb * TT: c * TOK + (tb + 1) * TT],
                                      sl(sc_bf, c % 2, TT), lnB_sb[:], ALU.add)
                      .then_inc(sem_vs, 1))
                v_ap = VS.inc()
                free[("xbf", c)] = (sem_vs, v_ap)
            ready[("applydone", tb)] = v_ap
        with nc.Block() as b1:
            P.emit(b1)

    if upto == "b1":
        return nc, _emit_dumps(nc, dumps, locals())

    tile = Ctr()
    widx = Ctr()

    def w_block_dma(P, img, j, width=H):
        s = widx.v % 2
        if ("wslot", s) in free:
            P.wait("gpsimd", *free[("wslot", s)])
        pair = dma(P, "gpsimd", semG, ctrG, s,
                   wstream[:, s * H:s * H + width], img[j])
        return pair, s

    def proj_block(P, ps_ring, img, j, bias_col, kind, out_sb, out_off, ring_n=4):
        wv, ws_slot = w_block_dma(P, img, j)
        widx.inc()
        for tb in range(NTB):
            t = tile.v
            tile.inc()
            slot = t % ring_n
            P.wait("tensor", *wv)
            if ("ps", slot) in free:
                P.wait("tensor", *free[("ps", slot)])
            for c in range(CH):
                st, sp = (c == 0), (c == CH - 1)
                if sp:
                    P.add("tensor", lambda te, c=c, s=slot, j2=ws_slot, tb=tb:
                          te.matmul(sl(ps_ring, s, TT),
                                    wstream[:, j2 * H + c * 128: j2 * H + (c + 1) * 128],
                                    xln1[:, c * TOK + tb * TT: c * TOK + (tb + 1) * TT],
                                    start=False, stop=True).then_inc(sem_ms, 1))
                else:
                    P.add("tensor", lambda te, c=c, s=slot, j2=ws_slot, tb=tb, st=st:
                          te.matmul(sl(ps_ring, s, TT),
                                    wstream[:, j2 * H + c * 128: j2 * H + (c + 1) * 128],
                                    xln1[:, c * TOK + tb * TT: c * TOK + (tb + 1) * TT],
                                    start=st, stop=False))
            m_grp = MS.inc()
            dst = out_sb[:, out_off + tb * TT: out_off + (tb + 1) * TT]
            if kind == "phi":
                P.wait("vector", sem_ms, m_grp)
                if ("sct", t % 2) in free:
                    P.wait("vector", *free[("sct", t % 2)])
                P.add("vector", lambda v, s=slot, i=t % 2, b=bias_col:
                      v.tensor_scalar(sl(sc_t, i, TT), sl(ps_ring, s, TT), b, 0.0,
                                      ALU.add, ALU.min))
                P.add("vector", lambda v, s=slot, i=t % 2, b=bias_col:
                      v.tensor_scalar(sl(sc_r, i, TT), sl(ps_ring, s, TT), b, 0.0,
                                      ALU.add, ALU.max).then_inc(sem_vs, 1))
                v_mm = VS.inc()
                P.wait("scalar", sem_vs, v_mm)
                if ("sce", t % 2) in free:
                    P.wait("scalar", *free[("sce", t % 2)])
                P.add("scalar", lambda s, i=t % 2:
                      s.activation(sl(sc_e, i, TT), sl(sc_t, i, TT), AF.Exp).then_inc(sem_ss, 1))
                s_exp = SS.inc()
                free[("sct", t % 2)] = (sem_ss, s_exp)
                P.wait("vector", sem_ss, s_exp)
                P.wait("vector", sem_vs, v_mm)
                P.add("vector", lambda v, i=t % 2, d=dst:
                      v.tensor_tensor(d, sl(sc_e, i, TT), sl(sc_r, i, TT), ALU.add)
                      .then_inc(sem_vs, 1))
                v_done = VS.inc()
                free[("sce", t % 2)] = (sem_vs, v_done)
                free[("ps", slot)] = (sem_vs, v_done)
                ready[("tile", t)] = (sem_vs, v_done)
            else:
                P.wait("scalar", sem_ms, m_grp)
                P.add("scalar", lambda s, sp2=slot, b=bias_col, d=dst:
                      s.activation(d, sl(ps_ring, sp2, TT), AF.Identity, bias=b, scale=1.0)
                      .then_inc(sem_ss, 1))
                s_done = SS.inc()
                free[("ps", slot)] = (sem_ss, s_done)
                ready[("tile", t)] = (sem_ss, s_done)
        free[("wslot", ws_slot)] = (sem_ms, m_grp)

    P = Prog()
    with (
        nc.psum_tensor("ps_ringA", [128, 2 * TT], F32) as ps_ringA,
        nc.psum_tensor("ps_tpA0", [128, 128], BF16) as ps_tpA0,
        nc.psum_tensor("ps_tpA1", [128, 128], BF16) as ps_tpA1,
        nc.psum_tensor("ps_tpB0", [128, 128], BF16) as ps_tpB0,
        nc.psum_tensor("ps_tpB1", [128, 128], BF16) as ps_tpB1,
        nc.psum_tensor("ps_kvA", [128, 128], F32) as ps_kvA,
        nc.psum_tensor("ps_kvB", [128, 128], F32) as ps_kvB,
    ):
        k_tiles = []
        for j in range(KVH):
            proj_block(P, ps_ringA, wimg["wk"], j, b_sb["bk"][:, j:j + 1], "phi", kphi,
                       j * TOK, ring_n=2)
            k_tiles += [tile.v - 2, tile.v - 1]
        v_tiles = []
        for j in range(KVH):
            proj_block(P, ps_ringA, wimg["wv"], j, b_sb["bv"][:, j:j + 1], "id", vbf,
                       j * TOK, ring_n=2)
            v_tiles += [tile.v - 2, tile.v - 1]
        P.wait("tensor", *ready[("tile", k_tiles[-1])])
        P.wait("tensor", *ready[("tile", v_tiles[-1])])
        kv_ps = [ps_kvA, ps_kvB]
        tpA = [ps_tpA0, ps_tpA1]
        tpB = [ps_tpB0, ps_tpB1]
        mkv = {}
        tpcopy = {}

        def emit_transposes(i):
            hh, c = i // 8, i % 8
            par = i % 2
            if ("tpA", par) in free:
                P.wait("tensor", *free[("tpA", par)])
            P.add("tensor", lambda te, hh=hh, c=c, par=par:
                  te.transpose(tpA[par][:],
                               kphi[:, hh * TOK + c * 128: hh * TOK + (c + 1) * 128],
                               ident_sb[:]).then_inc(sem_ms, 1))
            m_tk = MS.inc()
            P.add("tensor", lambda te, hh=hh, c=c, par=par:
                  te.transpose(tpB[par][:],
                               vbf[:, hh * TOK + c * 128: hh * TOK + (c + 1) * 128],
                               ident_sb[:]).then_inc(sem_ms, 1))
            m_tv = MS.inc()
            P.wait("vector", sem_ms, m_tk)
            if i >= 4:
                P.wait("vector", sem_ms, mkv.get(i - 4, 0))
            P.add("vector", lambda v, i=i, par=par: v.tensor_copy(sl(kt_r, i % 4, 128), tpA[par][:]))
            P.wait("vector", sem_ms, m_tv)
            P.add("vector", lambda v, i=i, par=par: v.tensor_copy(sl(vt_r, i % 4, 128), tpB[par][:])
                  .then_inc(sem_vs, 1))
            tpcopy[i] = VS.inc()
            free[("tpA", par)] = (sem_vs, tpcopy[i])

        def emit_kv_mm(i):
            hh, c = i // 8, i % 8
            P.wait("tensor", sem_vs, tpcopy[i])
            if c == 0 and ("kvps", hh % 2) in free:
                P.wait("tensor", *free[("kvps", hh % 2)])
            st, sp = (c == 0), (c == 7)
            P.add("tensor", lambda te, hh=hh, i=i, st=st, sp=sp:
                  te.matmul(kv_ps[hh % 2][:], sl(kt_r, i % 4, 128), sl(vt_r, i % 4, 128),
                            start=st, stop=sp).then_inc(sem_ms, 1))
            mkv[i] = MS.inc()
            if sp:
                P.wait("vector", sem_ms, mkv[i])
                P.add("vector", lambda v, hh=hh:
                      v.tensor_copy(ccin_sb[:, hh * 128:(hh + 1) * 128], kv_ps[hh % 2][:])
                      .then_inc(sem_vs, 1))
                free[("kvps", hh % 2)] = (sem_vs, VS.inc())

        emit_transposes(0)
        for i in range(1, KVH * 8):
            emit_transposes(i)
            emit_kv_mm(i - 1)
        emit_kv_mm(KVH * 8 - 1)
        for hh in range(KVH):
            if hh == KVH - 1:
                P.add("vector", lambda v, hh=hh:
                      v.reduce_sum(ccin_sb[:, KVH * 128 + hh: KVH * 128 + hh + 1],
                                   kphi[:, hh * TOK:(hh + 1) * TOK], axis=AX.X)
                      .then_inc(sem_vs, 1))
            else:
                P.add("vector", lambda v, hh=hh:
                      v.reduce_sum(ccin_sb[:, KVH * 128 + hh: KVH * 128 + hh + 1],
                                   kphi[:, hh * TOK:(hh + 1) * TOK], axis=AX.X))
        v_stage = VS.inc()
        P.wait("sync", sem_vs, v_stage)
        o_stage = dma(P, "sync", semO, ctrO, 0, cc_in[:], ccin_sb[:])
        ready[("ccstage",)] = o_stage
        with nc.Block() as b2a:
            P.emit(b2a)

    if upto == "b2a":
        return nc, _emit_dumps(nc, dumps, locals())

    P = Prog()
    with nc.psum_tensor("ps_ringB", [128, 4 * TT], F32) as ps_ringB:
        for k in range(4):
            free.pop(("ps", k), None)
        P.wait("sync", sem_cs, 1)
        x_cc = dma(P, "sync", semX, ctrX, 0, ccout_sb[:], cc_out[:])
        q_tiles = []
        post = [False]
        for j in range(QH):
            proj_block(P, ps_ringB, wimg["wq"], j, b_sb["bq"][:, j:j + 1], "phi", qphi, j * TOK)
            q_tiles += [tile.v - 2, tile.v - 1]
            if j == 3:
                P.wait("gpsimd", *ready[("ccstage",)])
                P.add("gpsimd", lambda g: g.collective_compute(
                    "AllReduce", ALU.add, replica_groups=[[0, 1, 2, 3], [4, 5, 6, 7]],
                    ins=[cc_in[:]], outs=[cc_out[:]]).then_inc(sem_cs, 1))
            if j == 10 and not post[0]:
                P.wait("vector", *x_cc)
                P.add("vector", lambda v: v.tensor_copy(kvt_bf[:], ccout_sb[:, :KVH * 128]))
                P.add("vector", lambda v: v.tensor_scalar(rn_tmp[:], ccout_sb[:, KVH * 128:],
                                                          EPS_ATT, None, ALU.add)
                      .then_inc(sem_vs, 1))
                v_rt = VS.inc()
                P.wait("vector", sem_vs, v_rt)
                P.add("vector", lambda v: v.reciprocal(rn_sb[:], rn_tmp[:]).then_inc(sem_vs, 1))
                ready[("kvrdy",)] = (sem_vs, VS.inc())
                post[0] = True
        P.wait("tensor", *ready[("kvrdy",)])
        for qh in range(QH):
            kvh = qh // 4
            for tb in range(NTB):
                t = tile.v
                tile.inc()
                slot = t % 4
                P.wait("tensor", *ready[("tile", q_tiles[qh * 2 + tb])])
                if ("ps", slot) in free:
                    P.wait("tensor", *free[("ps", slot)])
                P.add("tensor", lambda te, qh=qh, kvh=kvh, tb=tb, s=slot:
                      te.matmul(sl(ps_ringB, s, TT), kvt_bf[:, kvh * 128:(kvh + 1) * 128],
                                qphi[:, qh * TOK + tb * TT: qh * TOK + (tb + 1) * TT],
                                start=True, stop=True).then_inc(sem_ms, 1))
                m_at = MS.inc()
                P.wait("scalar", *ready[("kvrdy",)])
                P.wait("scalar", sem_ms, m_at)
                P.add("scalar", lambda s, qh=qh, kvh=kvh, tb=tb, sp=slot:
                      s.activation(qphi[:, qh * TOK + tb * TT: qh * TOK + (tb + 1) * TT],
                                   sl(ps_ringB, sp, TT), AF.Identity,
                                   bias=0.0, scale=rn_sb[:, kvh:kvh + 1]).then_inc(sem_ss, 1))
                s_at = SS.inc()
                free[("ps", slot)] = (sem_ss, s_at)
                ready[("attn", qh * 2 + tb)] = (sem_ss, s_at)
        with nc.Block() as b2b:
            P.emit(b2b)

    if upto == "b2b":
        return nc, _emit_dumps(nc, dumps, locals())

    P = Prog()
    with (
        nc.psum_tensor("ps_ringC", [128, 4 * TT], F32) as ps_ringC,
        nc.psum_tensor("ps2_s1a", [1, TT], F32) as ps2_s1a,
        nc.psum_tensor("ps2_s2a", [1, TT], F32) as ps2_s2a,
        nc.psum_tensor("ps2_s1b", [1, TT], F32) as ps2_s1b,
        nc.psum_tensor("ps2_s2b", [1, TT], F32) as ps2_s2b,
    ):
        for k in range(4):
            free.pop(("ps", k), None)
        P.wait("tensor", *ready[("attn", 63)])
        st_ps = [(ps2_s1a, ps2_s2a), (ps2_s1b, ps2_s2b)]
        x2bf_rdy, x2sq_rdy, x2bf_st = {}, {}, {}
        for s in range(CH * NTB):
            j, tb = s // 2, s % 2
            wv, ws_slot = w_block_dma(P, wimg["wo"], j)
            widx.inc()
            t = tile.v
            tile.inc()
            slot = t % 4
            P.wait("tensor", *wv)
            if ("ps", slot) in free:
                P.wait("tensor", *free[("ps", slot)])
            for c in range(CH):
                st, sp2 = (c == 0), (c == CH - 1)
                if sp2:
                    P.add("tensor", lambda te, c=c, s2=slot, j2=ws_slot, tb=tb:
                          te.matmul(sl(ps_ringC, s2, TT),
                                    wstream[:, j2 * H + c * 128: j2 * H + (c + 1) * 128],
                                    qphi[:, c * TOK + tb * TT: c * TOK + (tb + 1) * TT],
                                    start=False, stop=True).then_inc(sem_ms, 1))
                else:
                    P.add("tensor", lambda te, c=c, s2=slot, j2=ws_slot, tb=tb, st=st:
                          te.matmul(sl(ps_ringC, s2, TT),
                                    wstream[:, j2 * H + c * 128: j2 * H + (c + 1) * 128],
                                    qphi[:, c * TOK + tb * TT: c * TOK + (tb + 1) * TT],
                                    start=st, stop=False))
            m_grp = MS.inc()
            free[("wslot", ws_slot)] = (sem_ms, m_grp)
            if s >= 1:
                sp_, jp, tbp = s - 1, (s - 1) // 2, (s - 1) % 2
                P.wait("tensor", sem_vs, x2bf_rdy[sp_])
                P.wait("tensor", sem_ss, x2sq_rdy[sp_])
                P.add("tensor", lambda te, tbp=tbp, i=sp_ % 2, st=(jp == 0), sp2=(jp == CH - 1):
                      te.matmul(st_ps[tbp][0][:], ones_col[:, 0:1], sl(x2bf, i, TT),
                                start=st, stop=sp2))
                P.add("tensor", lambda te, tbp=tbp, st=(jp == 0), sp2=(jp == CH - 1):
                      te.matmul(st_ps[tbp][1][:], ones_col[:, 0:1], x2sq[:],
                                start=st, stop=sp2).then_inc(sem_ms, 1))
                ready[("x2stat", sp_)] = MS.inc()
            xsl = xre_ctr.v % 4
            xre_ctr.inc()
            if ("xre", xsl) in free:
                P.wait("sync", *free[("xre", xsl)])
            x_ld = dma(P, "sync", semX, ctrX, xsl, sl(xre, xsl, TT),
                       xT[j * 128:(j + 1) * 128, tb * TT:(tb + 1) * TT])
            P.wait("scalar", sem_ms, m_grp)
            if ("sct", s % 2) in free:
                P.wait("scalar", *free[("sct", s % 2)])
            P.add("scalar", lambda sc, s2=slot, j=j, i=s % 2:
                  sc.activation(sl(sc_t, i, TT), sl(ps_ringC, s2, TT), AF.Identity,
                                bias=b_sb["bo"][:, j:j + 1], scale=1.0).then_inc(sem_ss, 1))
            s_id = SS.inc()
            free[("ps", slot)] = (sem_ss, s_id)
            P.wait("vector", sem_ss, s_id)
            P.wait("vector", *x_ld)
            if ("sco", s % 2) in free:
                P.wait("vector", *free[("sco", s % 2)])
            P.add("vector", lambda v, i=s % 2, xs=xsl:
                  v.tensor_tensor(sl(sc_o, i, TT), sl(sc_t, i, TT), sl(xre, xs, TT), ALU.add)
                  .then_inc(sem_vs, 1))
            v_x2 = VS.inc()
            free[("sct", s % 2)] = (sem_vs, v_x2)
            free[("xre", xsl)] = (sem_vs, v_x2)
            P.wait("scalar", sem_vs, v_x2)
            if s >= 1:
                P.wait("scalar", sem_ms, ready[("x2stat", s - 1)])
            P.add("scalar", lambda sc, i=s % 2:
                  sc.activation(x2sq[:], sl(sc_o, i, TT), AF.Square).then_inc(sem_ss, 1))
            x2sq_rdy[s] = SS.inc()
            if s >= 2:
                P.wait("vector", sem_ms, ready[("x2stat", s - 2)])
                P.wait("vector", *x2bf_st[s - 2])
            P.wait("vector", sem_vs, v_x2)
            P.add("vector", lambda v, i=s % 2:
                  v.tensor_copy(sl(x2bf, i, TT), sl(sc_o, i, TT)).then_inc(sem_vs, 1))
            x2bf_rdy[s] = VS.inc()
            P.wait("sync", sem_vs, x2bf_rdy[s])
            x2bf_st[s] = dma(P, "sync", semB, ctrB, s % 2,
                             x2dbf[j * 128:(j + 1) * 128, tb * TT:(tb + 1) * TT],
                             sl(x2bf, s % 2, TT))
            P.wait("sync", sem_vs, v_x2)
            o_x2 = dma(P, "sync", semO, ctrO, s % 2,
                       x2d[j * 128:(j + 1) * 128, tb * TT:(tb + 1) * TT], sl(sc_o, i := s % 2, TT))
            free[("sco", s % 2)] = o_x2
        sp_ = CH * NTB - 1
        tbp = sp_ % 2
        P.wait("tensor", sem_vs, x2bf_rdy[sp_])
        P.wait("tensor", sem_ss, x2sq_rdy[sp_])
        P.add("tensor", lambda te, tbp=tbp, i=sp_ % 2:
              te.matmul(st_ps[tbp][0][:], ones_col[:, 0:1], sl(x2bf, i, TT),
                        start=False, stop=True))
        P.add("tensor", lambda te, tbp=tbp:
              te.matmul(st_ps[tbp][1][:], ones_col[:, 0:1], x2sq[:],
                        start=False, stop=True).then_inc(sem_ms, 1))
        m_last = MS.inc()

        def ln2_fin(P, tb, stats_ms):
            inv = 1.0 / H
            P.wait("vector", sem_ms, stats_ms)
            P.add("vector", lambda v, tb=tb:
                  v.tensor_scalar(row_mean[:], st_ps[tb][0][:], inv, None, ALU.mult))
            P.add("vector", lambda v, tb=tb:
                  v.tensor_scalar(row_var[:], st_ps[tb][1][:], inv, None, ALU.mult)
                  .then_inc(sem_vs, 1))
            v1 = VS.inc()
            P.wait("vector", sem_vs, v1)
            P.add("vector", lambda v: v.tensor_tensor(row_r[:], row_mean[:], row_mean[:], ALU.mult)
                  .then_inc(sem_vs, 1))
            v2 = VS.inc()
            P.wait("vector", sem_vs, v2)
            P.add("vector", lambda v: v.tensor_tensor(row_var[:], row_var[:], row_r[:], ALU.subtract)
                  .then_inc(sem_vs, 1))
            v_var = VS.inc()
            P.wait("scalar", sem_vs, v_var)
            P.add("scalar", lambda sc: sc.activation(row_b[:], row_var[:], AF.Ln,
                                                     bias=np.float32(EPS_LN), scale=1.0)
                  .then_inc(sem_ss, 1))
            s_ln = SS.inc()
            P.wait("scalar", sem_ss, s_ln)
            P.add("scalar", lambda sc: sc.activation(row_r[:], row_b[:], AF.Exp, bias=0.0, scale=-0.5)
                  .then_inc(sem_ss, 1))
            s_r = SS.inc()
            P.wait("vector", sem_ss, s_r)
            P.add("vector", lambda v: v.scalar_tensor_tensor(row_b[:], row_mean[:], -1.0, row_r[:],
                                                             ALU.mult, ALU.mult).then_inc(sem_vs, 1))
            v4 = VS.inc()
            P.add("vector", lambda v, tb=tb: v.tensor_copy(ln2a[tb][:], row_r[:]))
            P.wait("vector", sem_vs, v4)
            P.add("vector", lambda v, tb=tb: v.tensor_copy(ln2b[tb][:], row_b[:]).then_inc(sem_vs, 1))
            return VS.inc()

        ln2_fin(P, 0, ready[("x2stat", CH * NTB - 2)])
        ln2_fin(P, 1, m_last)
        P.add("sync", lambda sy, v0=ctrO[0].v, v1=ctrO[1].v, b0=ctrB[0].v, b1=ctrB[1].v:
              (sy.wait_ge(semO[0], v0), sy.wait_ge(semO[1], v1),
               sy.wait_ge(semB[0], b0), sy.wait_ge(semB[1], b1)))
        with nc.Block() as b3:
            P.emit(b3)

    if upto == "b3":
        return nc, _emit_dumps(nc, dumps, locals())

    P = Prog()
    with (
        nc.psum_tensor("ps_g", [128, 2 * TOK], F32) as ps_g,
        nc.psum_tensor("ps_u", [128, 2 * TOK], F32) as ps_u,
    ):
        for k in range(4):
            free.pop(("ps", k), None)
        free.pop(("wslot", 0), None)
        free.pop(("wslot", 1), None)

        for tb in range(NTB):
            P.add("tensor", lambda te, tb=tb:
                  te.matmul(ps_g[:, tb * TT:(tb + 1) * TT], ones_row[:], ln2a[tb][:],
                            start=True, stop=True))
            P.add("tensor", lambda te, tb=tb:
                  te.matmul(ps_g[:, TOK + tb * TT: TOK + (tb + 1) * TT], ones_row[:],
                            ln2b[tb][:], start=True, stop=True).then_inc(sem_ms, 1))
            MS.inc()
        P.wait("vector", sem_ms, MS.v)
        P.add("vector", lambda v: v.tensor_copy(lnA2[:], ps_g[:, 0:TOK]))
        P.add("vector", lambda v: v.tensor_copy(lnB2[:], ps_g[:, TOK:2 * TOK])
              .then_inc(sem_vs, 1))
        v_ab = VS.inc()
        P.wait("vector", sem_vs, v_ab)

        apply_v = {}
        for c in range(CH):
            asl = c % 2
            if ("x2a", asl) in free:
                P.wait("sync", *free[("x2a", asl)])
            x_ld = dma(P, "sync", semA, ctrA, asl, x2a[:, asl * TOK:(asl + 1) * TOK],
                       x2dbf[c * 128:(c + 1) * 128, :])
            P.wait("vector", *x_ld)
            P.add("vector", lambda v, i=asl, c=c:
                  v.tensor_tensor(sl(scbf2, c % 2, TOK), x2a[:, i * TOK:(i + 1) * TOK],
                                  lnA2[:], ALU.mult).then_inc(sem_vs, 1))
            v_t = VS.inc()
            P.wait("vector", sem_vs, v_t)
            P.add("vector", lambda v, c=c:
                  v.tensor_tensor(xln2[:, c * TOK:(c + 1) * TOK], sl(scbf2, c % 2, TOK),
                                  lnB2[:], ALU.add).then_inc(sem_vs, 1))
            v_ap = VS.inc()
            apply_v[c] = v_ap
            free[("x2a", asl)] = (sem_vs, v_ap)

        h_st = {}
        stag = [stg0, stg1, stg2]
        stg2_used = [False]

        def gu_half_dma(jt, st_i, half):
            t = (jt * 2 + st_i) * 2 + half
            slot = t % 3
            if slot == 2 and not stg2_used[0]:
                stg2_used[0] = True
                P.wait("sync", sem_vs, apply_v[CH - 1])
            if ("stg", slot) in free:
                P.wait("sync", *free[("stg", slot)])
            img = (wimg["wg"] if st_i == 0 else wimg["wu"])[jt]
            return dma(P, "sync", semE, ctrE, slot, stag[slot][:],
                       img[:, half * 2048:(half + 1) * 2048])

        def gu_half_cast(jt, st_i, half, wv):
            t = (jt * 2 + st_i) * 2 + half
            slot = t % 3
            gslot = st_i * 2 + (jt % 2)
            eng = "scalar" if st_i == 0 else "vector"
            P.wait(eng, *wv)
            if half == 0 and ("gu", gslot) in free:
                P.wait(eng, *free[("gu", gslot)])
            dst = gu[:, gslot * H + half * 2048: gslot * H + (half + 1) * 2048]
            if st_i == 0:
                P.add("scalar", lambda sc, dst=dst, slot=slot:
                      sc.activation(dst, stag[slot][:], AF.Identity,
                                    bias=0.0, scale=1.0).then_inc(sem_ss, 1))
                v = SS.inc()
                free[("stg", slot)] = (sem_ss, v)
            else:
                P.add("vector", lambda vv, dst=dst, slot=slot:
                      vv.tensor_copy(dst, stag[slot][:]).then_inc(sem_vs, 1))
                v = VS.inc()
                free[("stg", slot)] = (sem_vs, v)
            return v

        def _dec(t):
            return t // 4, (t // 2) % 2, t % 2

        dmas = {}
        cast_v = {}
        for t0 in range(3):
            dmas[t0] = gu_half_dma(*_dec(t0))
        for jt in range(IT):
            pslot = jt % 2
            for k in range(4):
                t = 4 * jt + k
                _, st_i, half = _dec(t)
                cast_v[t] = gu_half_cast(jt, st_i, half, dmas.pop(t))
                if t + 3 < 4 * IT:
                    dmas[t + 3] = gu_half_dma(*_dec(t + 3))
                if k == 1:
                    P.wait("tensor", sem_ss, cast_v[4 * jt + 1])
                    if ("psG", pslot) in free:
                        P.wait("tensor", *free[("psG", pslot)])
                    gs = 0 * 2 + jt % 2
                    for half2 in range(NTB):
                        for c in range(CH):
                            if jt == 0 and half2 == 0:
                                P.wait("tensor", sem_vs, apply_v[c])
                            lastg = (half2 == NTB - 1 and c == CH - 1)
                            P.add("tensor", lambda te, c=c, h2=half2, ps=pslot, gs=gs,
                                  st=(c == 0), sp=(c == CH - 1), l=lastg:
                                  te.matmul(ps_g[:, ps * TOK + h2 * TT: ps * TOK + h2 * TT + TT],
                                            gu[:, gs * H + c * 128: gs * H + (c + 1) * 128],
                                            xln2[:, c * TOK + h2 * TT: c * TOK + h2 * TT + TT],
                                            start=st, stop=sp)
                                  .then_inc(sem_ms, 1) if l else
                                  te.matmul(ps_g[:, ps * TOK + h2 * TT: ps * TOK + h2 * TT + TT],
                                            gu[:, gs * H + c * 128: gs * H + (c + 1) * 128],
                                            xln2[:, c * TOK + h2 * TT: c * TOK + h2 * TT + TT],
                                            start=st, stop=sp))
                    m_gate = MS.inc()
                    free[("gu", 0 * 2 + jt % 2)] = (sem_ms, m_gate)
                if k == 3:
                    P.wait("tensor", sem_vs, cast_v[4 * jt + 3])
                    if ("psU", pslot) in free:
                        P.wait("tensor", *free[("psU", pslot)])
                    gs = 1 * 2 + jt % 2
                    for half2 in range(NTB):
                        for c in range(CH):
                            lastu = (half2 == NTB - 1 and c == CH - 1)
                            P.add("tensor", lambda te, c=c, h2=half2, ps=pslot, gs=gs,
                                  st=(c == 0), sp=(c == CH - 1), l=lastu:
                                  te.matmul(ps_u[:, ps * TOK + h2 * TT: ps * TOK + h2 * TT + TT],
                                            gu[:, gs * H + c * 128: gs * H + (c + 1) * 128],
                                            xln2[:, c * TOK + h2 * TT: c * TOK + h2 * TT + TT],
                                            start=st, stop=sp)
                                  .then_inc(sem_ms, 1) if l else
                                  te.matmul(ps_u[:, ps * TOK + h2 * TT: ps * TOK + h2 * TT + TT],
                                            gu[:, gs * H + c * 128: gs * H + (c + 1) * 128],
                                            xln2[:, c * TOK + h2 * TT: c * TOK + h2 * TT + TT],
                                            start=st, stop=sp))
                    m_up = MS.inc()
                    free[("gu", 1 * 2 + jt % 2)] = (sem_ms, m_up)
            P.wait("scalar", sem_ms, m_gate)
            if ("scr2", jt % 2) in free:
                P.wait("scalar", *free[("scr2", jt % 2)])
            P.add("scalar", lambda s, ps=pslot, jt=jt, i=jt % 2:
                  s.activation(sl(scr2, i, TOK), ps_g[:, ps * TOK:(ps + 1) * TOK], AF.Silu,
                               bias=0.0, scale=b_sb["ag"][:, jt:jt + 1]).then_inc(sem_ss, 1))
            s_silu = SS.inc()
            free[("psG", pslot)] = (sem_ss, s_silu)
            P.wait("vector", sem_ms, m_up)
            if ("sce2", jt % 2) in free:
                P.wait("vector", *free[("sce2", jt % 2)])
            P.add("vector", lambda v, ps=pslot, jt=jt, i=jt % 2:
                  v.tensor_scalar(sl(sce2, i, TOK), ps_u[:, ps * TOK:(ps + 1) * TOK],
                                  b_sb["au"][:, jt:jt + 1], None, ALU.mult)
                  .then_inc(sem_vs, 1))
            v_u = VS.inc()
            free[("psU", pslot)] = (sem_vs, v_u)
            hsl = jt % 4
            P.wait("vector", sem_ss, s_silu)
            P.wait("vector", sem_vs, v_u)
            if ("hr", hsl) in free:
                P.wait("vector", *free[("hr", hsl)])
            P.add("vector", lambda v, jt=jt, i=jt % 2, hsl=hsl:
                  v.tensor_tensor(sl(hr, hsl, TOK), sl(scr2, i, TOK), sl(sce2, i, TOK),
                                  ALU.mult).then_inc(sem_vs, 1))
            v_h = VS.inc()
            free[("scr2", jt % 2)] = (sem_vs, v_h)
            free[("sce2", jt % 2)] = (sem_vs, v_h)
            P.wait("sync", sem_vs, v_h)
            h_st[jt] = dma(P, "sync", semW, ctrW, hsl,
                           hd[jt * 128:(jt + 1) * 128, :], sl(hr, hsl, TOK))
            free[("hr", hsl)] = h_st[jt]

        for k in range(4):
            if ("xre", k) in free:
                P.wait("sync", *free[("xre", k)])
                free.pop(("xre", k))
        for k in range(2):
            P.wait("tensor", *free[("psG", k)])
            P.wait("tensor", *free[("psU", k)])
            free.pop(("psG", k))
            free.pop(("psU", k))
        dn_ps = [ps_g, ps_g, ps_u, ps_u]
        NGT = (CH // 4) * IT
        cast_done = {}

        def dn_wd_dma(gt):
            ob, jt = gt // IT, gt % IT
            slot = gt % 4
            if ("stag", slot) in free:
                P.wait("scalar", *free[("stag", slot)])
            return dma(P, "scalar", semD, ctrD, slot, sl(wdf, slot, 512),
                       wimg["wdob"][ob][:, jt * 512:(jt + 1) * 512])

        def dn_wd_cast(gt, wv):
            slot = gt % 4
            P.wait("scalar", *wv)
            if ("wdsb", slot) in free:
                P.wait("scalar", *free[("wdsb", slot)])
            P.add("scalar", lambda sc, slot=slot:
                  sc.activation(sl(wds, slot, 512), sl(wdf, slot, 512), AF.Identity,
                                bias=0.0, scale=1.0).then_inc(sem_ss, 1))
            cast_done[gt] = SS.inc()
            free[("stag", slot)] = (sem_ss, cast_done[gt])

        wdv = {gt: dn_wd_dma(gt) for gt in range(2)}
        for gt in range(NGT):
            ob, jt = gt // IT, gt % IT
            slot = gt % 4
            if gt + 2 < NGT:
                wdv[gt + 2] = dn_wd_dma(gt + 2)
            dn_wd_cast(gt, wdv.pop(gt))
            hsl = gt % 4
            if ob == 0 and jt < 2 and ("x2a", hsl) in free:
                P.wait("sync", *free[("x2a", hsl)])
            if ("hr", hsl) in free:
                P.wait("sync", *free[("hr", hsl)])
            P.wait("sync", *h_st[jt])
            hv = dma(P, "sync", semA, ctrA, hsl, sl(hr, hsl, TOK),
                     hd[jt * 128:(jt + 1) * 128, :])
            P.wait("tensor", sem_ss, cast_done[gt])
            P.wait("tensor", *hv)
            for oi in range(4):
                if jt == 0 and ("dnps", oi) in free:
                    P.wait("tensor", *free[("dnps", oi)])
                for half in range(2):
                    lastm = (oi == 3 and half == 1)
                    P.add("tensor", lambda te, oi=oi, h2=half, wsl=slot, hsl=hsl,
                          st=(jt == 0), sp=(jt == IT - 1), l=lastm:
                          te.matmul(dn_ps[oi][:, (oi % 2) * TOK + h2 * TT:
                                              (oi % 2) * TOK + h2 * TT + TT],
                                    wds[:, wsl * 512 + oi * 128: wsl * 512 + (oi + 1) * 128],
                                    hr[:, hsl * TOK + h2 * TT: hsl * TOK + h2 * TT + TT],
                                    start=st, stop=sp)
                          .then_inc(sem_ms, 1) if l else
                          te.matmul(dn_ps[oi][:, (oi % 2) * TOK + h2 * TT:
                                              (oi % 2) * TOK + h2 * TT + TT],
                                    wds[:, wsl * 512 + oi * 128: wsl * 512 + (oi + 1) * 128],
                                    hr[:, hsl * TOK + h2 * TT: hsl * TOK + h2 * TT + TT],
                                    start=st, stop=sp))
            m_gt = MS.inc()
            free[("wdsb", slot)] = (sem_ms, m_gt)
            free[("hr", hsl)] = (sem_ms, m_gt)
            if jt != IT - 1:
                continue
            m_dn = m_gt
            for oi in range(4):
                o = ob * 4 + oi
                esl = o % 2
                P.wait("vector", sem_ms, m_dn)
                if ("dnt", esl) in free:
                    P.wait("vector", *free[("dnt", esl)])
                P.add("vector", lambda v, oi=oi, o=o, i=esl:
                      v.tensor_scalar(sl(dn_t, i, TOK), dn_ps[oi][:, (oi % 2) * TOK:
                                                                  (oi % 2 + 1) * TOK],
                                      b_sb["ad"][:, o:o + 1], None, ALU.mult)
                      .then_inc(sem_vs, 1))
                v_scale = VS.inc()
                free[("dnps", oi)] = (sem_vs, v_scale)
                xsl = o % 2
                if ("xre2", xsl) in free:
                    P.wait("sync", *free[("xre2", xsl)])
                x_ld = dma(P, "sync", semX, ctrX, xsl, xre[:, xsl * TOK:(xsl + 1) * TOK],
                           x2d[o * 128:(o + 1) * 128, :])
                P.wait("vector", sem_vs, v_scale)
                P.wait("vector", *x_ld)
                if ("dno", esl) in free:
                    P.wait("vector", *free[("dno", esl)])
                P.add("vector", lambda v, i=esl, xs=xsl:
                      v.tensor_tensor(sl(dn_o, i, TOK), sl(dn_t, i, TOK),
                                      xre[:, xs * TOK:(xs + 1) * TOK], ALU.add)
                      .then_inc(sem_vs, 1))
                v_out = VS.inc()
                free[("dnt", esl)] = (sem_vs, v_out)
                free[("xre2", xsl)] = (sem_vs, v_out)
                P.wait("sync", sem_vs, v_out)
                o_out = dma(P, "sync", semO, ctrO, o % 2,
                            outT[o * 128:(o + 1) * 128, :], sl(dn_o, esl, TOK))
                free[("dno", esl)] = o_out
        P.add("sync", lambda sy, v0=ctrO[0].v, v1=ctrO[1].v:
              (sy.wait_ge(semO[0], v0), sy.wait_ge(semO[1], v1)))
        with nc.Block() as b4:
            P.emit(b4)

    return nc, _emit_dumps(nc, dumps, locals())


def _emit_dumps(nc, dumps, env):
    if not dumps:
        return []
    names = []
    with nc.semaphore("dmp") as dsem, nc.Block() as bd:
        def _(sy):
            n = 0
            for nm in dumps:
                src = env[nm]
                ap = src if isinstance(src, bass.AP) else src.ap() if hasattr(src, "ap") else src[:]
                shape = list(ap.tensor.shape) if hasattr(ap, "tensor") else list(src.shape)
                dt = ap.dtype
                out = nc.dram_tensor("dump_" + nm, shape, dt, kind="ExternalOutput").ap()
                sy.dma_start(out[:], ap).then_inc(dsem, 16)
                n += 16
                names.append("dump_" + nm)
            sy.wait_ge(dsem, n)
        bd.sync(_)
    return names


_NC_CACHE = {}


def _prep_inputs(inputs):
    x = np.asarray(inputs["L0_attn_hidden_states"], dtype=np.float32)
    xf = np.ascontiguousarray(x.reshape(2 * 4096, H))

    def img(w, nblk, wide):
        w = np.asarray(w, dtype=np.float32)
        return np.ascontiguousarray(
            w.reshape(nblk, 128, wide // 128, 128).transpose(0, 3, 2, 1).reshape(nblk, 128, wide))

    def cols(v, n):
        return np.ascontiguousarray(np.asarray(v, dtype=np.float32).reshape(n, 128).T)

    common = {
        "wq": img(inputs["L0_attn_Wq"], QH, H),
        "wk": img(inputs["L0_attn_Wk"], KVH, H),
        "wv": img(inputs["L0_attn_Wv"], KVH, H),
        "wo": img(inputs["L0_attn_Wo"], CH, H),
        "wg": img(inputs["L0_ffn_Wg"], IT, H),
        "wu": img(inputs["L0_ffn_Wu"], IT, H),
        "wdob": img(inputs["L0_ffn_Wd"], CH, IT * 128)
                .reshape(CH // 4, 4, 128, IT, 128).transpose(0, 2, 3, 1, 4)
                .reshape(CH // 4, 128, IT * 512).copy(),
        "bq": cols(inputs["L0_attn_bq"], QH),
        "bk": cols(inputs["L0_attn_bk"], KVH),
        "bv": cols(inputs["L0_attn_bv"], KVH),
        "bo": cols(inputs["L0_attn_bo"], CH),
        "ag": cols(inputs["L0_ffn_ag"], IT),
        "au": cols(inputs["L0_ffn_au"], IT),
        "ad": cols(inputs["L0_ffn_ad"], CH),
        "ident": np.eye(128, dtype=ml_dtypes.bfloat16),
    }
    in_maps = []
    for c in range(8):
        m = dict(common)
        m["xT"] = np.ascontiguousarray(xf[c * TOK:(c + 1) * TOK].T)
        in_maps.append(m)
    return in_maps


def kernel(**inputs):
    in_maps = _prep_inputs(inputs)
    if "nc" not in _NC_CACHE:
        _NC_CACHE["nc"] = build()[0]
    nc = _NC_CACHE["nc"]
    res = run_bass_kernel_spmd(nc, in_maps, core_ids=list(range(8)))
    parts = [res.results[c]["outT"].T for c in range(8)]
    out = np.concatenate(parts, axis=0).reshape(2, 4096, H)
    return np.ascontiguousarray(out, dtype=np.float32)


# version 09
# speedup vs baseline: 1.0863x; 1.0083x over previous
import numpy as np
import ml_dtypes
import concourse.bass as bass
import concourse.mybir as mybir
from concourse.bass_utils import run_bass_kernel_spmd

F32 = mybir.dt.float32
BF16 = mybir.dt.bfloat16
AF = mybir.ActivationFunctionType
ALU = mybir.AluOpType
AX = mybir.AxisListType

H = 4096
CH = 32
TOK = 1024
TT = 512
NTB = 2
KVH = 8
QH = 32
IT = 86
EPS_LN = 1.0e-5
EPS_ATT = 1.0e-10
CCW = KVH * 128 + KVH

ENGS = ("sync", "gpsimd", "tensor", "scalar", "vector")


class Prog:
    def __init__(self):
        self.q = {e: [] for e in ENGS}
        self.hi = {e: {} for e in ENGS}

    def add(self, eng, fn):
        self.q[eng].append(fn)

    def wait(self, eng, sem, val):
        if val <= 0:
            return
        key = id(sem)
        if self.hi[eng].get(key, -1) >= val:
            return
        self.hi[eng][key] = val
        self.q[eng].append(lambda e, s=sem, v=val: e.wait_ge(s, v))

    def emit(self, block):
        for name in ENGS:
            ops = self.q[name]
            if not ops:
                continue
            getattr(block, name)(lambda e, ops=ops: [f(e) for f in ops])


class Ctr:
    def __init__(self):
        self.v = 0

    def inc(self, n=1):
        self.v += n
        return self.v


def build(upto="b4", dumps=()):
    nc = bass.Bass("TRN2", target_bir_lowering=False, debug=False)

    xT = nc.dram_tensor("xT", [H, TOK], F32, kind="ExternalInput").ap()
    wimg = {}
    for nm, nblk, wide in (("wq", QH, H), ("wk", KVH, H), ("wv", KVH, H),
                           ("wo", CH, H), ("wg", IT, H), ("wu", IT, H),
                           ("wdob", CH // 4, IT * 512)):
        wimg[nm] = nc.dram_tensor(nm, [nblk, 128, wide], F32, kind="ExternalInput").ap()
    bvecs = {}
    for nm, w in (("bq", QH), ("bk", KVH), ("bv", KVH), ("bo", CH),
                  ("ag", IT), ("au", IT), ("ad", CH)):
        bvecs[nm] = nc.dram_tensor(nm, [128, w], F32, kind="ExternalInput").ap()
    ident_in = nc.dram_tensor("ident", [128, 128], BF16, kind="ExternalInput").ap()
    outT = nc.dram_tensor("outT", [H, TOK], F32, kind="ExternalOutput").ap()
    x2d = nc.dram_tensor("x2d", [H, TOK], F32, kind="Internal").ap()
    x2dbf = nc.dram_tensor("x2dbf", [H, TOK], BF16, kind="Internal").ap()
    hd = nc.dram_tensor("hd", [IT * 128, TOK], BF16, kind="Internal").ap()
    cc_in = nc.dram_tensor("cc_in", [128, CCW], F32, kind="Internal").ap()
    cc_out = nc.dram_tensor("cc_out", [128, CCW], F32, kind="Internal").ap()

    sem_ms = nc.alloc_semaphore("s_ms")
    sem_vs = nc.alloc_semaphore("s_vs")
    sem_ss = nc.alloc_semaphore("s_ss")
    sem_cs = nc.alloc_semaphore("s_cs")
    MS, VS, SS = Ctr(), Ctr(), Ctr()
    semG = [nc.alloc_semaphore(f"s_g{i}") for i in range(4)]
    ctrG = [Ctr() for _ in range(4)]
    semX = [nc.alloc_semaphore(f"s_x{i}") for i in range(4)]
    ctrX = [Ctr() for _ in range(4)]
    semO = [nc.alloc_semaphore(f"s_o{i}") for i in range(2)]
    ctrO = [Ctr() for _ in range(2)]
    semA = [nc.alloc_semaphore(f"s_a{i}") for i in range(4)]
    ctrA = [Ctr() for _ in range(4)]
    semB = [nc.alloc_semaphore(f"s_b{i}") for i in range(2)]
    ctrB = [Ctr() for _ in range(2)]
    semW = [nc.alloc_semaphore(f"s_w{i}") for i in range(4)]
    ctrW = [Ctr() for _ in range(4)]
    semD = [nc.alloc_semaphore(f"s_d{i}") for i in range(4)]
    ctrD = [Ctr() for _ in range(4)]
    semE = [nc.alloc_semaphore(f"s_e{i}") for i in range(3)]
    ctrE = [Ctr() for _ in range(3)]

    ident_sb = nc.alloc_sbuf_tensor("ident_sb", [128, 128], BF16)
    ones_col = nc.alloc_sbuf_tensor("ones_col", [128, 16], BF16)
    ones_row = nc.alloc_sbuf_tensor("ones_row", [1, 128], BF16)
    b_sb = {nm: nc.alloc_sbuf_tensor(nm + "_sb", [128, w], F32)
            for nm, w in (("bq", QH), ("bk", KVH), ("bv", KVH), ("bo", CH),
                          ("ag", IT), ("au", IT), ("ad", CH))}
    ln2a = [nc.alloc_sbuf_tensor(f"ln2a{t}", [1, TT], BF16) for t in range(NTB)]
    ln2b = [nc.alloc_sbuf_tensor(f"ln2b{t}", [1, TT], BF16) for t in range(NTB)]
    sc_t = nc.alloc_sbuf_tensor("sc_t", [128, 2 * TT], F32)
    sc_e = nc.alloc_sbuf_tensor("sc_e", [128, 2 * TT], BF16)
    sc_r = nc.alloc_sbuf_tensor("sc_r", [128, 2 * TT], BF16)
    sc_bf = nc.alloc_sbuf_tensor("sc_bf", [128, 2 * TT], BF16)
    lnA_sb = nc.alloc_sbuf_tensor("lnA_sb", [128, TT], BF16)
    lnB_sb = nc.alloc_sbuf_tensor("lnB_sb", [128, TT], BF16)
    sc_o = nc.alloc_sbuf_tensor("sc_o", [128, 2 * TT], F32)
    xre = nc.alloc_sbuf_tensor("xre", [128, 4 * TT], F32)
    wstream = nc.alloc_sbuf_tensor("wstream", [128, IT * 128], BF16)
    kvt_bf = nc.alloc_sbuf_tensor("kvt_bf", [128, KVH * 128], BF16)
    rn_sb = nc.alloc_sbuf_tensor("rn_sb", [128, KVH], F32)
    rn_tmp = nc.alloc_sbuf_tensor("rn_tmp", [128, KVH], F32)
    x2a = nc.alloc_sbuf_tensor("x2a", [128, 4 * TT], BF16)

    TOP = 229344
    MB_BYTES = 88064
    MA_BYTES = 65536
    MB0 = (TOP - MB_BYTES) // 32 * 32
    MA0 = (MB0 - MA_BYTES) // 32 * 32
    A = lambda name, shape, dt, off: nc.alloc_sbuf_tensor_at(name, shape, dt, offset=off)
    xln1 = A("xln1", [128, CH * TOK], BF16, MA0)
    xln2 = A("xln2", [128, CH * TOK], BF16, MA0)
    gu = A("gu", [128, 4 * H], BF16, MB0)
    wdf = A("wdf", [128, 4 * 512], F32, MB0)
    stg0 = A("stg0", [128, 2048], F32, MB0 + 45056)
    stg1 = A("stg1", [128, 2048], F32, MB0 + 53248)
    stg2 = A("stg2", [128, 2048], F32, MB0 + 61440)
    hr = A("hr", [128, 4 * TOK], BF16, MB0 + 32768)
    wds = A("wds", [128, 4 * 512], BF16, MB0 + 40960)
    dn_t = A("dn_t", [128, 2 * TOK], F32, MB0 + 45056)
    dn_o = A("dn_o", [128, 2 * TOK], F32, MB0 + 53248)
    lnA2 = A("lnA2", [128, TOK], BF16, MB0 + 61440)
    lnB2 = A("lnB2", [128, TOK], BF16, MB0 + 63488)
    scbf2 = A("scbf2", [128, 2 * TOK], BF16, MB0 + 65536)
    scr2 = A("scr2", [128, 2 * TOK], BF16, MB0 + 69632)
    sce2 = A("sce2", [128, 2 * TOK], BF16, MB0 + 73728)
    xbf_full = A("xbf_full", [128, CH * TT], BF16, MB0)
    xsq_r = A("xsq_r", [128, 4 * TT], BF16, MB0 + 32768)
    kphi = A("kphi", [128, KVH * TOK], BF16, MB0)
    vbf = A("vbf", [128, KVH * TOK], BF16, MB0 + 16384)
    kt_r = A("kt_r", [128, 4 * 128], BF16, MB0 + 32768)
    vt_r = A("vt_r", [128, 4 * 128], BF16, MB0 + 33792)
    qphi = A("qphi", [128, QH * TOK], BF16, MB0)
    row_mean = A("row_mean", [1, TT], F32, MB0 + 65536)
    row_var = A("row_var", [1, TT], F32, MB0 + 67584)
    row_r = A("row_r", [1, TT], F32, MB0 + 69632)
    row_a_bf = A("row_a_bf", [1, TT], BF16, MB0 + 71680)
    row_b_bf = A("row_b_bf", [1, TT], BF16, MB0 + 72704)
    ccin_sb = A("ccin_sb", [128, CCW], F32, MB0 + 73728)
    ccout_sb = A("ccout_sb", [128, CCW], F32, MB0 + 77888)
    x2bf = A("x2bf", [128, 2 * TT], BF16, MB0 + 82048)
    x2sq = A("x2sq", [128, TT], BF16, MB0 + 84096)
    row_b = A("row_b", [1, TT], F32, MB0 + 85120)

    assert nc.sbuf_base <= MA0, (nc.sbuf_base, MA0)

    def sl(t, i, w):
        return t[:, i * w:(i + 1) * w]

    free = {}
    ready = {}
    xre_ctr = Ctr()

    def dma(P, eng, sems, ctrs, idx, dst, src):
        P.add(eng, lambda e, d=dst, s=src, sm=sems[idx]: e.dma_start(d, s).then_inc(sm, 16))
        return (sems[idx], ctrs[idx].inc(16))

    P = Prog()
    _b0_target = ctrX[0].v + 8 * 16
    def b0_sync(sy, v_target=_b0_target):
        sy.dma_start(ident_sb[:], ident_in[:]).then_inc(semX[0], 16)
        for nm in ("bq", "bk", "bv", "bo", "ag", "au", "ad"):
            sy.dma_start(b_sb[nm][:], bvecs[nm][:]).then_inc(semX[0], 16)
        sy.wait_ge(semX[0], v_target)
    P.add("sync", b0_sync)
    ctrX[0].inc(8 * 16)
    P.add("vector", lambda v: v.memset(ones_col[:], 1.0))
    P.add("vector", lambda v: v.memset(ones_row[:], 1.0))
    with nc.Block() as b0:
        P.emit(b0)

    def ln_finalize(P, s1_ps, s2_ps, a_dst, b_dst, stats_done_ms):
        inv = 1.0 / H
        P.wait("vector", sem_ms, stats_done_ms)
        P.add("vector", lambda v: v.tensor_scalar(row_mean[:], s1_ps[:], inv, None, ALU.mult))
        P.add("vector", lambda v: v.tensor_scalar(row_var[:], s2_ps[:], inv, None, ALU.mult)
              .then_inc(sem_vs, 1))
        v1 = VS.inc()
        P.wait("vector", sem_vs, v1)
        P.add("vector", lambda v: v.tensor_tensor(row_r[:], row_mean[:], row_mean[:], ALU.mult)
              .then_inc(sem_vs, 1))
        v2 = VS.inc()
        P.wait("vector", sem_vs, v2)
        P.add("vector", lambda v: v.tensor_tensor(row_var[:], row_var[:], row_r[:], ALU.subtract)
              .then_inc(sem_vs, 1))
        v_var = VS.inc()
        P.wait("scalar", sem_vs, v_var)
        P.add("scalar", lambda s: s.activation(row_b[:], row_var[:], AF.Ln,
                                               bias=np.float32(EPS_LN), scale=1.0)
              .then_inc(sem_ss, 1))
        s_ln = SS.inc()
        P.wait("scalar", sem_ss, s_ln)
        P.add("scalar", lambda s: s.activation(row_r[:], row_b[:], AF.Exp, bias=0.0, scale=-0.5)
              .then_inc(sem_ss, 1))
        s_r = SS.inc()
        P.wait("vector", sem_ss, s_r)
        P.add("vector", lambda v: v.scalar_tensor_tensor(row_b[:], row_mean[:], -1.0, row_r[:],
                                                         ALU.mult, ALU.mult).then_inc(sem_vs, 1))
        v4 = VS.inc()
        P.add("vector", lambda v, a=a_dst: v.tensor_copy(a[:], row_r[:]))
        P.wait("vector", sem_vs, v4)
        P.add("vector", lambda v, b=b_dst: v.tensor_copy(b[:], row_b[:]).then_inc(sem_vs, 1))
        return VS.inc()

    P = Prog()
    with (
        nc.psum_tensor("ps_s1", [1, TT], F32) as ps_s1,
        nc.psum_tensor("ps_s2", [1, TT], F32) as ps_s2,
        nc.psum_tensor("ps_abc", [128, TT], F32) as ps_abc,
        nc.psum_tensor("ps_bbc", [128, TT], F32) as ps_bbc,
    ):
        sq_vals = {}
        for tb in range(NTB):
            for c in range(CH):
                gsl = c % 4
                if tb * CH + c >= 4:
                    P.wait("gpsimd", sem_ss, sq_vals[tb * CH + c - 4])
                if ("xbf", c) in free:
                    P.wait("gpsimd", *free[("xbf", c)])
                w_xbf = dma(P, "gpsimd", semG, ctrG, gsl, sl(xbf_full, c, TT),
                            xT[c * 128:(c + 1) * 128, tb * TT:(tb + 1) * TT])
                P.wait("scalar", *w_xbf)
                if ("xsq", gsl) in free:
                    P.wait("scalar", *free[("xsq", gsl)])
                P.add("scalar", lambda s, i=gsl, c=c:
                      s.activation(sl(xsq_r, i, TT), sl(xbf_full, c, TT), AF.Square)
                      .then_inc(sem_ss, 1))
                s_sq = SS.inc()
                sq_vals[tb * CH + c] = s_sq
                P.wait("tensor", *w_xbf)
                P.wait("tensor", sem_ss, s_sq)
                if tb == 1 and c == 0:
                    P.wait("tensor", sem_vs, ready[("lnfin1", 0)])
                P.add("tensor", lambda t, c=c, st=(c == 0), sp=(c == CH - 1):
                      t.matmul(ps_s1[:], ones_col[:, 0:1], sl(xbf_full, c, TT), start=st, stop=sp))
                P.add("tensor", lambda t, i=gsl, st=(c == 0), sp=(c == CH - 1):
                      t.matmul(ps_s2[:], ones_col[:, 0:1], sl(xsq_r, i, TT), start=st, stop=sp)
                      .then_inc(sem_ms, 1))
                m_chunk = MS.inc()
                free[("xsq", gsl)] = (sem_ms, m_chunk)
            ready[("lnfin1", tb)] = VS.v + 1
            rows_ready = ln_finalize(P, ps_s1, ps_s2, row_a_bf, row_b_bf, m_chunk)
            P.wait("tensor", sem_vs, rows_ready)
            if tb == 1:
                P.wait("tensor", sem_vs, ready[("applydone", 0)])
            P.add("tensor", lambda t: t.matmul(ps_abc[:], ones_row[:], row_a_bf[:],
                                               start=True, stop=True))
            P.add("tensor", lambda t: t.matmul(ps_bbc[:], ones_row[:], row_b_bf[:],
                                               start=True, stop=True).then_inc(sem_ms, 1))
            m_bc = MS.inc()
            P.wait("vector", sem_ms, m_bc)
            if tb == 1:
                P.wait("vector", sem_vs, ready[("applydone", 0)])
            P.add("vector", lambda v: v.tensor_copy(lnA_sb[:], ps_abc[:]))
            P.add("vector", lambda v: v.tensor_copy(lnB_sb[:], ps_bbc[:]).then_inc(sem_vs, 1))
            v_ab = VS.inc()
            P.wait("vector", sem_vs, v_ab)
            for c in range(CH):
                P.add("vector", lambda v, c=c:
                      v.tensor_tensor(sl(sc_bf, c % 2, TT), sl(xbf_full, c, TT), lnA_sb[:], ALU.mult)
                      .then_inc(sem_vs, 1))
                v_t = VS.inc()
                P.wait("vector", sem_vs, v_t)
                P.add("vector", lambda v, c=c, tb=tb:
                      v.tensor_tensor(xln1[:, c * TOK + tb * TT: c * TOK + (tb + 1) * TT],
                                      sl(sc_bf, c % 2, TT), lnB_sb[:], ALU.add)
                      .then_inc(sem_vs, 1))
                v_ap = VS.inc()
                free[("xbf", c)] = (sem_vs, v_ap)
            ready[("applydone", tb)] = v_ap
        with nc.Block() as b1:
            P.emit(b1)

    if upto == "b1":
        return nc, _emit_dumps(nc, dumps, locals())

    tile = Ctr()
    widx = Ctr()

    def w_block_dma(P, img, j, width=H):
        s = widx.v % 2
        if ("wslot", s) in free:
            P.wait("gpsimd", *free[("wslot", s)])
        pair = dma(P, "gpsimd", semG, ctrG, s,
                   wstream[:, s * H:s * H + width], img[j])
        return pair, s

    def proj_block(P, ps_ring, img, j, bias_col, kind, out_sb, out_off, ring_n=4):
        wv, ws_slot = w_block_dma(P, img, j)
        widx.inc()
        for tb in range(NTB):
            t = tile.v
            tile.inc()
            slot = t % ring_n
            P.wait("tensor", *wv)
            if ("ps", slot) in free:
                P.wait("tensor", *free[("ps", slot)])
            for c in range(CH):
                st, sp = (c == 0), (c == CH - 1)
                if sp:
                    P.add("tensor", lambda te, c=c, s=slot, j2=ws_slot, tb=tb:
                          te.matmul(sl(ps_ring, s, TT),
                                    wstream[:, j2 * H + c * 128: j2 * H + (c + 1) * 128],
                                    xln1[:, c * TOK + tb * TT: c * TOK + (tb + 1) * TT],
                                    start=False, stop=True).then_inc(sem_ms, 1))
                else:
                    P.add("tensor", lambda te, c=c, s=slot, j2=ws_slot, tb=tb, st=st:
                          te.matmul(sl(ps_ring, s, TT),
                                    wstream[:, j2 * H + c * 128: j2 * H + (c + 1) * 128],
                                    xln1[:, c * TOK + tb * TT: c * TOK + (tb + 1) * TT],
                                    start=st, stop=False))
            m_grp = MS.inc()
            dst = out_sb[:, out_off + tb * TT: out_off + (tb + 1) * TT]
            if kind == "phi":
                P.wait("vector", sem_ms, m_grp)
                if ("sct", t % 2) in free:
                    P.wait("vector", *free[("sct", t % 2)])
                P.add("vector", lambda v, s=slot, i=t % 2, b=bias_col:
                      v.tensor_scalar(sl(sc_t, i, TT), sl(ps_ring, s, TT), b, 0.0,
                                      ALU.add, ALU.min))
                P.add("vector", lambda v, s=slot, i=t % 2, b=bias_col:
                      v.tensor_scalar(sl(sc_r, i, TT), sl(ps_ring, s, TT), b, 0.0,
                                      ALU.add, ALU.max).then_inc(sem_vs, 1))
                v_mm = VS.inc()
                P.wait("scalar", sem_vs, v_mm)
                if ("sce", t % 2) in free:
                    P.wait("scalar", *free[("sce", t % 2)])
                P.add("scalar", lambda s, i=t % 2:
                      s.activation(sl(sc_e, i, TT), sl(sc_t, i, TT), AF.Exp).then_inc(sem_ss, 1))
                s_exp = SS.inc()
                free[("sct", t % 2)] = (sem_ss, s_exp)
                P.wait("vector", sem_ss, s_exp)
                P.wait("vector", sem_vs, v_mm)
                P.add("vector", lambda v, i=t % 2, d=dst:
                      v.tensor_tensor(d, sl(sc_e, i, TT), sl(sc_r, i, TT), ALU.add)
                      .then_inc(sem_vs, 1))
                v_done = VS.inc()
                free[("sce", t % 2)] = (sem_vs, v_done)
                free[("ps", slot)] = (sem_vs, v_done)
                ready[("tile", t)] = (sem_vs, v_done)
            else:
                P.wait("scalar", sem_ms, m_grp)
                P.add("scalar", lambda s, sp2=slot, b=bias_col, d=dst:
                      s.activation(d, sl(ps_ring, sp2, TT), AF.Identity, bias=b, scale=1.0)
                      .then_inc(sem_ss, 1))
                s_done = SS.inc()
                free[("ps", slot)] = (sem_ss, s_done)
                ready[("tile", t)] = (sem_ss, s_done)
        free[("wslot", ws_slot)] = (sem_ms, m_grp)

    P = Prog()
    with (
        nc.psum_tensor("ps_ringA", [128, 2 * TT], F32) as ps_ringA,
        nc.psum_tensor("ps_tpA0", [128, 128], BF16) as ps_tpA0,
        nc.psum_tensor("ps_tpA1", [128, 128], BF16) as ps_tpA1,
        nc.psum_tensor("ps_tpB0", [128, 128], BF16) as ps_tpB0,
        nc.psum_tensor("ps_tpB1", [128, 128], BF16) as ps_tpB1,
        nc.psum_tensor("ps_kvA", [128, 128], F32) as ps_kvA,
        nc.psum_tensor("ps_kvB", [128, 128], F32) as ps_kvB,
    ):
        k_tiles = []
        for j in range(KVH):
            proj_block(P, ps_ringA, wimg["wk"], j, b_sb["bk"][:, j:j + 1], "phi", kphi,
                       j * TOK, ring_n=2)
            k_tiles += [tile.v - 2, tile.v - 1]
        v_tiles = []
        for j in range(KVH):
            proj_block(P, ps_ringA, wimg["wv"], j, b_sb["bv"][:, j:j + 1], "id", vbf,
                       j * TOK, ring_n=2)
            v_tiles += [tile.v - 2, tile.v - 1]
        P.wait("tensor", *ready[("tile", k_tiles[-1])])
        P.wait("tensor", *ready[("tile", v_tiles[-1])])
        kv_ps = [ps_kvA, ps_kvB]
        tpA = [ps_tpA0, ps_tpA1]
        tpB = [ps_tpB0, ps_tpB1]
        mkv = {}
        tpcopy = {}

        def emit_transposes(i):
            hh, c = i // 8, i % 8
            par = i % 2
            if ("tpA", par) in free:
                P.wait("tensor", *free[("tpA", par)])
            P.add("tensor", lambda te, hh=hh, c=c, par=par:
                  te.transpose(tpA[par][:],
                               kphi[:, hh * TOK + c * 128: hh * TOK + (c + 1) * 128],
                               ident_sb[:]).then_inc(sem_ms, 1))
            m_tk = MS.inc()
            P.add("tensor", lambda te, hh=hh, c=c, par=par:
                  te.transpose(tpB[par][:],
                               vbf[:, hh * TOK + c * 128: hh * TOK + (c + 1) * 128],
                               ident_sb[:]).then_inc(sem_ms, 1))
            m_tv = MS.inc()
            P.wait("vector", sem_ms, m_tk)
            if i >= 4:
                P.wait("vector", sem_ms, mkv.get(i - 4, 0))
            P.add("vector", lambda v, i=i, par=par: v.tensor_copy(sl(kt_r, i % 4, 128), tpA[par][:]))
            P.wait("vector", sem_ms, m_tv)
            P.add("vector", lambda v, i=i, par=par: v.tensor_copy(sl(vt_r, i % 4, 128), tpB[par][:])
                  .then_inc(sem_vs, 1))
            tpcopy[i] = VS.inc()
            free[("tpA", par)] = (sem_vs, tpcopy[i])

        def emit_kv_mm(i):
            hh, c = i // 8, i % 8
            P.wait("tensor", sem_vs, tpcopy[i])
            if c == 0 and ("kvps", hh % 2) in free:
                P.wait("tensor", *free[("kvps", hh % 2)])
            st, sp = (c == 0), (c == 7)
            P.add("tensor", lambda te, hh=hh, i=i, st=st, sp=sp:
                  te.matmul(kv_ps[hh % 2][:], sl(kt_r, i % 4, 128), sl(vt_r, i % 4, 128),
                            start=st, stop=sp).then_inc(sem_ms, 1))
            mkv[i] = MS.inc()
            if sp:
                P.wait("vector", sem_ms, mkv[i])
                P.add("vector", lambda v, hh=hh:
                      v.tensor_copy(ccin_sb[:, hh * 128:(hh + 1) * 128], kv_ps[hh % 2][:])
                      .then_inc(sem_vs, 1))
                free[("kvps", hh % 2)] = (sem_vs, VS.inc())

        emit_transposes(0)
        for i in range(1, KVH * 8):
            emit_transposes(i)
            emit_kv_mm(i - 1)
        emit_kv_mm(KVH * 8 - 1)
        for hh in range(KVH):
            if hh == KVH - 1:
                P.add("vector", lambda v, hh=hh:
                      v.reduce_sum(ccin_sb[:, KVH * 128 + hh: KVH * 128 + hh + 1],
                                   kphi[:, hh * TOK:(hh + 1) * TOK], axis=AX.X)
                      .then_inc(sem_vs, 1))
            else:
                P.add("vector", lambda v, hh=hh:
                      v.reduce_sum(ccin_sb[:, KVH * 128 + hh: KVH * 128 + hh + 1],
                                   kphi[:, hh * TOK:(hh + 1) * TOK], axis=AX.X))
        v_stage = VS.inc()
        P.wait("sync", sem_vs, v_stage)
        o_stage = dma(P, "sync", semO, ctrO, 0, cc_in[:], ccin_sb[:])
        ready[("ccstage",)] = o_stage
        with nc.Block() as b2a:
            P.emit(b2a)

    if upto == "b2a":
        return nc, _emit_dumps(nc, dumps, locals())

    P = Prog()
    with nc.psum_tensor("ps_ringB", [128, 4 * TT], F32) as ps_ringB:
        for k in range(4):
            free.pop(("ps", k), None)
        P.wait("sync", sem_cs, 1)
        x_cc = dma(P, "sync", semX, ctrX, 0, ccout_sb[:], cc_out[:])
        q_tiles = []
        post = [False]
        for j in range(QH):
            proj_block(P, ps_ringB, wimg["wq"], j, b_sb["bq"][:, j:j + 1], "phi", qphi, j * TOK)
            q_tiles += [tile.v - 2, tile.v - 1]
            if j == 3:
                P.wait("gpsimd", *ready[("ccstage",)])
                P.add("gpsimd", lambda g: g.collective_compute(
                    "AllReduce", ALU.add, replica_groups=[[0, 1, 2, 3], [4, 5, 6, 7]],
                    ins=[cc_in[:]], outs=[cc_out[:]]).then_inc(sem_cs, 1))
            if j == 10 and not post[0]:
                P.wait("vector", *x_cc)
                P.add("vector", lambda v: v.tensor_copy(kvt_bf[:], ccout_sb[:, :KVH * 128]))
                P.add("vector", lambda v: v.tensor_scalar(rn_tmp[:], ccout_sb[:, KVH * 128:],
                                                          EPS_ATT, None, ALU.add)
                      .then_inc(sem_vs, 1))
                v_rt = VS.inc()
                P.wait("vector", sem_vs, v_rt)
                P.add("vector", lambda v: v.reciprocal(rn_sb[:], rn_tmp[:]).then_inc(sem_vs, 1))
                ready[("kvrdy",)] = (sem_vs, VS.inc())
                post[0] = True
        P.wait("tensor", *ready[("kvrdy",)])
        for qh in range(QH):
            kvh = qh // 4
            for tb in range(NTB):
                t = tile.v
                tile.inc()
                slot = t % 4
                P.wait("tensor", *ready[("tile", q_tiles[qh * 2 + tb])])
                if ("ps", slot) in free:
                    P.wait("tensor", *free[("ps", slot)])
                P.add("tensor", lambda te, qh=qh, kvh=kvh, tb=tb, s=slot:
                      te.matmul(sl(ps_ringB, s, TT), kvt_bf[:, kvh * 128:(kvh + 1) * 128],
                                qphi[:, qh * TOK + tb * TT: qh * TOK + (tb + 1) * TT],
                                start=True, stop=True).then_inc(sem_ms, 1))
                m_at = MS.inc()
                P.wait("scalar", *ready[("kvrdy",)])
                P.wait("scalar", sem_ms, m_at)
                P.add("scalar", lambda s, qh=qh, kvh=kvh, tb=tb, sp=slot:
                      s.activation(qphi[:, qh * TOK + tb * TT: qh * TOK + (tb + 1) * TT],
                                   sl(ps_ringB, sp, TT), AF.Identity,
                                   bias=0.0, scale=rn_sb[:, kvh:kvh + 1]).then_inc(sem_ss, 1))
                s_at = SS.inc()
                free[("ps", slot)] = (sem_ss, s_at)
                ready[("attn", qh * 2 + tb)] = (sem_ss, s_at)
        with nc.Block() as b2b:
            P.emit(b2b)

    if upto == "b2b":
        return nc, _emit_dumps(nc, dumps, locals())

    P = Prog()
    with (
        nc.psum_tensor("ps_ringC", [128, 4 * TT], F32) as ps_ringC,
        nc.psum_tensor("ps2_s1a", [1, TT], F32) as ps2_s1a,
        nc.psum_tensor("ps2_s2a", [1, TT], F32) as ps2_s2a,
        nc.psum_tensor("ps2_s1b", [1, TT], F32) as ps2_s1b,
        nc.psum_tensor("ps2_s2b", [1, TT], F32) as ps2_s2b,
    ):
        for k in range(4):
            free.pop(("ps", k), None)
        P.wait("tensor", *ready[("attn", 63)])
        st_ps = [(ps2_s1a, ps2_s2a), (ps2_s1b, ps2_s2b)]
        x2bf_rdy, x2sq_rdy, x2bf_st = {}, {}, {}
        for s in range(CH * NTB):
            j, tb = s // 2, s % 2
            wv, ws_slot = w_block_dma(P, wimg["wo"], j)
            widx.inc()
            t = tile.v
            tile.inc()
            slot = t % 4
            P.wait("tensor", *wv)
            if ("ps", slot) in free:
                P.wait("tensor", *free[("ps", slot)])
            for c in range(CH):
                st, sp2 = (c == 0), (c == CH - 1)
                if sp2:
                    P.add("tensor", lambda te, c=c, s2=slot, j2=ws_slot, tb=tb:
                          te.matmul(sl(ps_ringC, s2, TT),
                                    wstream[:, j2 * H + c * 128: j2 * H + (c + 1) * 128],
                                    qphi[:, c * TOK + tb * TT: c * TOK + (tb + 1) * TT],
                                    start=False, stop=True).then_inc(sem_ms, 1))
                else:
                    P.add("tensor", lambda te, c=c, s2=slot, j2=ws_slot, tb=tb, st=st:
                          te.matmul(sl(ps_ringC, s2, TT),
                                    wstream[:, j2 * H + c * 128: j2 * H + (c + 1) * 128],
                                    qphi[:, c * TOK + tb * TT: c * TOK + (tb + 1) * TT],
                                    start=st, stop=False))
            m_grp = MS.inc()
            free[("wslot", ws_slot)] = (sem_ms, m_grp)
            if s >= 1:
                sp_, jp, tbp = s - 1, (s - 1) // 2, (s - 1) % 2
                P.wait("tensor", sem_vs, x2bf_rdy[sp_])
                P.wait("tensor", sem_ss, x2sq_rdy[sp_])
                P.add("tensor", lambda te, tbp=tbp, i=sp_ % 2, st=(jp == 0), sp2=(jp == CH - 1):
                      te.matmul(st_ps[tbp][0][:], ones_col[:, 0:1], sl(x2bf, i, TT),
                                start=st, stop=sp2))
                P.add("tensor", lambda te, tbp=tbp, st=(jp == 0), sp2=(jp == CH - 1):
                      te.matmul(st_ps[tbp][1][:], ones_col[:, 0:1], x2sq[:],
                                start=st, stop=sp2).then_inc(sem_ms, 1))
                ready[("x2stat", sp_)] = MS.inc()
            xsl = xre_ctr.v % 4
            xre_ctr.inc()
            if ("xre", xsl) in free:
                P.wait("sync", *free[("xre", xsl)])
            x_ld = dma(P, "sync", semX, ctrX, xsl, sl(xre, xsl, TT),
                       xT[j * 128:(j + 1) * 128, tb * TT:(tb + 1) * TT])
            P.wait("scalar", sem_ms, m_grp)
            if ("sct", s % 2) in free:
                P.wait("scalar", *free[("sct", s % 2)])
            P.add("scalar", lambda sc, s2=slot, j=j, i=s % 2:
                  sc.activation(sl(sc_t, i, TT), sl(ps_ringC, s2, TT), AF.Identity,
                                bias=b_sb["bo"][:, j:j + 1], scale=1.0).then_inc(sem_ss, 1))
            s_id = SS.inc()
            free[("ps", slot)] = (sem_ss, s_id)
            P.wait("vector", sem_ss, s_id)
            P.wait("vector", *x_ld)
            if ("sco", s % 2) in free:
                P.wait("vector", *free[("sco", s % 2)])
            P.add("vector", lambda v, i=s % 2, xs=xsl:
                  v.tensor_tensor(sl(sc_o, i, TT), sl(sc_t, i, TT), sl(xre, xs, TT), ALU.add)
                  .then_inc(sem_vs, 1))
            v_x2 = VS.inc()
            free[("sct", s % 2)] = (sem_vs, v_x2)
            free[("xre", xsl)] = (sem_vs, v_x2)
            P.wait("scalar", sem_vs, v_x2)
            if s >= 1:
                P.wait("scalar", sem_ms, ready[("x2stat", s - 1)])
            P.add("scalar", lambda sc, i=s % 2:
                  sc.activation(x2sq[:], sl(sc_o, i, TT), AF.Square).then_inc(sem_ss, 1))
            x2sq_rdy[s] = SS.inc()
            if s >= 2:
                P.wait("vector", sem_ms, ready[("x2stat", s - 2)])
                P.wait("vector", *x2bf_st[s - 2])
            P.wait("vector", sem_vs, v_x2)
            P.add("vector", lambda v, i=s % 2:
                  v.tensor_copy(sl(x2bf, i, TT), sl(sc_o, i, TT)).then_inc(sem_vs, 1))
            x2bf_rdy[s] = VS.inc()
            P.wait("sync", sem_vs, x2bf_rdy[s])
            x2bf_st[s] = dma(P, "sync", semB, ctrB, s % 2,
                             x2dbf[j * 128:(j + 1) * 128, tb * TT:(tb + 1) * TT],
                             sl(x2bf, s % 2, TT))
            P.wait("sync", sem_vs, v_x2)
            o_x2 = dma(P, "sync", semO, ctrO, s % 2,
                       x2d[j * 128:(j + 1) * 128, tb * TT:(tb + 1) * TT], sl(sc_o, i := s % 2, TT))
            free[("sco", s % 2)] = o_x2
        sp_ = CH * NTB - 1
        tbp = sp_ % 2
        P.wait("tensor", sem_vs, x2bf_rdy[sp_])
        P.wait("tensor", sem_ss, x2sq_rdy[sp_])
        P.add("tensor", lambda te, tbp=tbp, i=sp_ % 2:
              te.matmul(st_ps[tbp][0][:], ones_col[:, 0:1], sl(x2bf, i, TT),
                        start=False, stop=True))
        P.add("tensor", lambda te, tbp=tbp:
              te.matmul(st_ps[tbp][1][:], ones_col[:, 0:1], x2sq[:],
                        start=False, stop=True).then_inc(sem_ms, 1))
        m_last = MS.inc()

        def ln2_fin(P, tb, stats_ms):
            inv = 1.0 / H
            P.wait("vector", sem_ms, stats_ms)
            P.add("vector", lambda v, tb=tb:
                  v.tensor_scalar(row_mean[:], st_ps[tb][0][:], inv, None, ALU.mult))
            P.add("vector", lambda v, tb=tb:
                  v.tensor_scalar(row_var[:], st_ps[tb][1][:], inv, None, ALU.mult)
                  .then_inc(sem_vs, 1))
            v1 = VS.inc()
            P.wait("vector", sem_vs, v1)
            P.add("vector", lambda v: v.tensor_tensor(row_r[:], row_mean[:], row_mean[:], ALU.mult)
                  .then_inc(sem_vs, 1))
            v2 = VS.inc()
            P.wait("vector", sem_vs, v2)
            P.add("vector", lambda v: v.tensor_tensor(row_var[:], row_var[:], row_r[:], ALU.subtract)
                  .then_inc(sem_vs, 1))
            v_var = VS.inc()
            P.wait("scalar", sem_vs, v_var)
            P.add("scalar", lambda sc: sc.activation(row_b[:], row_var[:], AF.Ln,
                                                     bias=np.float32(EPS_LN), scale=1.0)
                  .then_inc(sem_ss, 1))
            s_ln = SS.inc()
            P.wait("scalar", sem_ss, s_ln)
            P.add("scalar", lambda sc: sc.activation(row_r[:], row_b[:], AF.Exp, bias=0.0, scale=-0.5)
                  .then_inc(sem_ss, 1))
            s_r = SS.inc()
            P.wait("vector", sem_ss, s_r)
            P.add("vector", lambda v: v.scalar_tensor_tensor(row_b[:], row_mean[:], -1.0, row_r[:],
                                                             ALU.mult, ALU.mult).then_inc(sem_vs, 1))
            v4 = VS.inc()
            P.add("vector", lambda v, tb=tb: v.tensor_copy(ln2a[tb][:], row_r[:]))
            P.wait("vector", sem_vs, v4)
            P.add("vector", lambda v, tb=tb: v.tensor_copy(ln2b[tb][:], row_b[:]).then_inc(sem_vs, 1))
            return VS.inc()

        ln2_fin(P, 0, ready[("x2stat", CH * NTB - 2)])
        ln2_fin(P, 1, m_last)
        P.add("sync", lambda sy, v0=ctrO[0].v, v1=ctrO[1].v, b0=ctrB[0].v, b1=ctrB[1].v:
              (sy.wait_ge(semO[0], v0), sy.wait_ge(semO[1], v1),
               sy.wait_ge(semB[0], b0), sy.wait_ge(semB[1], b1)))
        with nc.Block() as b3:
            P.emit(b3)

    if upto == "b3":
        return nc, _emit_dumps(nc, dumps, locals())

    P = Prog()
    with (
        nc.psum_tensor("ps_g", [128, 2 * TOK], F32) as ps_g,
        nc.psum_tensor("ps_u", [128, 2 * TOK], F32) as ps_u,
    ):
        for k in range(4):
            free.pop(("ps", k), None)
        free.pop(("wslot", 0), None)
        free.pop(("wslot", 1), None)

        for tb in range(NTB):
            P.add("tensor", lambda te, tb=tb:
                  te.matmul(ps_g[:, tb * TT:(tb + 1) * TT], ones_row[:], ln2a[tb][:],
                            start=True, stop=True))
            P.add("tensor", lambda te, tb=tb:
                  te.matmul(ps_g[:, TOK + tb * TT: TOK + (tb + 1) * TT], ones_row[:],
                            ln2b[tb][:], start=True, stop=True).then_inc(sem_ms, 1))
            MS.inc()
        P.wait("vector", sem_ms, MS.v)
        P.add("vector", lambda v: v.tensor_copy(lnA2[:], ps_g[:, 0:TOK]))
        P.add("vector", lambda v: v.tensor_copy(lnB2[:], ps_g[:, TOK:2 * TOK])
              .then_inc(sem_vs, 1))
        v_ab = VS.inc()
        P.wait("vector", sem_vs, v_ab)

        apply_v = {}
        for c in range(CH):
            asl = c % 2
            if ("x2a", asl) in free:
                P.wait("sync", *free[("x2a", asl)])
            x_ld = dma(P, "sync", semA, ctrA, asl, x2a[:, asl * TOK:(asl + 1) * TOK],
                       x2dbf[c * 128:(c + 1) * 128, :])
            P.wait("vector", *x_ld)
            P.add("vector", lambda v, i=asl, c=c:
                  v.tensor_tensor(sl(scbf2, c % 2, TOK), x2a[:, i * TOK:(i + 1) * TOK],
                                  lnA2[:], ALU.mult).then_inc(sem_vs, 1))
            v_t = VS.inc()
            P.wait("vector", sem_vs, v_t)
            P.add("vector", lambda v, c=c:
                  v.tensor_tensor(xln2[:, c * TOK:(c + 1) * TOK], sl(scbf2, c % 2, TOK),
                                  lnB2[:], ALU.add).then_inc(sem_vs, 1))
            v_ap = VS.inc()
            apply_v[c] = v_ap
            free[("x2a", asl)] = (sem_vs, v_ap)

        h_st = {}
        stag = [stg0, stg1, stg2]
        stg2_used = [False]

        def gu_half_dma(jt, st_i, half):
            t = (jt * 2 + st_i) * 2 + half
            slot = t % 3
            if slot == 2 and not stg2_used[0]:
                stg2_used[0] = True
                P.wait("sync", sem_vs, apply_v[CH - 1])
            if ("stg", slot) in free:
                P.wait("sync", *free[("stg", slot)])
            img = (wimg["wg"] if st_i == 0 else wimg["wu"])[jt]
            return dma(P, "sync", semE, ctrE, slot, stag[slot][:],
                       img[:, half * 2048:(half + 1) * 2048])

        def gu_half_cast(jt, st_i, half, wv):
            t = (jt * 2 + st_i) * 2 + half
            slot = t % 3
            gslot = st_i * 2 + (jt % 2)
            eng = "scalar" if st_i == 0 else "vector"
            P.wait(eng, *wv)
            if half == 0 and ("gu", gslot) in free:
                P.wait(eng, *free[("gu", gslot)])
            dst = gu[:, gslot * H + half * 2048: gslot * H + (half + 1) * 2048]
            if st_i == 0:
                P.add("scalar", lambda sc, dst=dst, slot=slot:
                      sc.activation(dst, stag[slot][:], AF.Identity,
                                    bias=0.0, scale=1.0).then_inc(sem_ss, 1))
                v = SS.inc()
                free[("stg", slot)] = (sem_ss, v)
            else:
                P.add("vector", lambda vv, dst=dst, slot=slot:
                      vv.tensor_copy(dst, stag[slot][:]).then_inc(sem_vs, 1))
                v = VS.inc()
                free[("stg", slot)] = (sem_vs, v)
            return v

        def _dec(t):
            return t // 4, (t // 2) % 2, t % 2

        dmas = {}
        cast_v = {}
        for t0 in range(3):
            dmas[t0] = gu_half_dma(*_dec(t0))
        for jt in range(IT):
            pslot = jt % 2
            for k in range(4):
                t = 4 * jt + k
                _, st_i, half = _dec(t)
                cast_v[t] = gu_half_cast(jt, st_i, half, dmas.pop(t))
                if t + 3 < 4 * IT:
                    dmas[t + 3] = gu_half_dma(*_dec(t + 3))
                if k == 1:
                    P.wait("tensor", sem_ss, cast_v[4 * jt + 1])
                    if ("psG", pslot) in free:
                        P.wait("tensor", *free[("psG", pslot)])
                    gs = 0 * 2 + jt % 2
                    for half2 in range(NTB):
                        for c in range(CH):
                            if jt == 0 and half2 == 0:
                                P.wait("tensor", sem_vs, apply_v[c])
                            lastg = (half2 == NTB - 1 and c == CH - 1)
                            P.add("tensor", lambda te, c=c, h2=half2, ps=pslot, gs=gs,
                                  st=(c == 0), sp=(c == CH - 1), l=lastg:
                                  te.matmul(ps_g[:, ps * TOK + h2 * TT: ps * TOK + h2 * TT + TT],
                                            gu[:, gs * H + c * 128: gs * H + (c + 1) * 128],
                                            xln2[:, c * TOK + h2 * TT: c * TOK + h2 * TT + TT],
                                            start=st, stop=sp)
                                  .then_inc(sem_ms, 1) if l else
                                  te.matmul(ps_g[:, ps * TOK + h2 * TT: ps * TOK + h2 * TT + TT],
                                            gu[:, gs * H + c * 128: gs * H + (c + 1) * 128],
                                            xln2[:, c * TOK + h2 * TT: c * TOK + h2 * TT + TT],
                                            start=st, stop=sp))
                    m_gate = MS.inc()
                    free[("gu", 0 * 2 + jt % 2)] = (sem_ms, m_gate)
                if k == 3:
                    P.wait("tensor", sem_vs, cast_v[4 * jt + 3])
                    if ("psU", pslot) in free:
                        P.wait("tensor", *free[("psU", pslot)])
                    gs = 1 * 2 + jt % 2
                    for half2 in range(NTB):
                        for c in range(CH):
                            lastu = (half2 == NTB - 1 and c == CH - 1)
                            P.add("tensor", lambda te, c=c, h2=half2, ps=pslot, gs=gs,
                                  st=(c == 0), sp=(c == CH - 1), l=lastu:
                                  te.matmul(ps_u[:, ps * TOK + h2 * TT: ps * TOK + h2 * TT + TT],
                                            gu[:, gs * H + c * 128: gs * H + (c + 1) * 128],
                                            xln2[:, c * TOK + h2 * TT: c * TOK + h2 * TT + TT],
                                            start=st, stop=sp)
                                  .then_inc(sem_ms, 1) if l else
                                  te.matmul(ps_u[:, ps * TOK + h2 * TT: ps * TOK + h2 * TT + TT],
                                            gu[:, gs * H + c * 128: gs * H + (c + 1) * 128],
                                            xln2[:, c * TOK + h2 * TT: c * TOK + h2 * TT + TT],
                                            start=st, stop=sp))
                    m_up = MS.inc()
                    free[("gu", 1 * 2 + jt % 2)] = (sem_ms, m_up)
            P.wait("scalar", sem_ms, m_gate)
            if ("scr2", jt % 2) in free:
                P.wait("scalar", *free[("scr2", jt % 2)])
            P.add("scalar", lambda s, ps=pslot, jt=jt, i=jt % 2:
                  s.activation(sl(scr2, i, TOK), ps_g[:, ps * TOK:(ps + 1) * TOK], AF.Silu,
                               bias=0.0, scale=b_sb["ag"][:, jt:jt + 1]).then_inc(sem_ss, 1))
            s_silu = SS.inc()
            free[("psG", pslot)] = (sem_ss, s_silu)
            P.wait("vector", sem_ms, m_up)
            if ("sce2", jt % 2) in free:
                P.wait("vector", *free[("sce2", jt % 2)])
            P.add("vector", lambda v, ps=pslot, jt=jt, i=jt % 2:
                  v.tensor_scalar(sl(sce2, i, TOK), ps_u[:, ps * TOK:(ps + 1) * TOK],
                                  b_sb["au"][:, jt:jt + 1], None, ALU.mult)
                  .then_inc(sem_vs, 1))
            v_u = VS.inc()
            free[("psU", pslot)] = (sem_vs, v_u)
            hsl = jt % 4
            P.wait("vector", sem_ss, s_silu)
            P.wait("vector", sem_vs, v_u)
            if ("hr", hsl) in free:
                P.wait("vector", *free[("hr", hsl)])
            P.add("vector", lambda v, jt=jt, i=jt % 2, hsl=hsl:
                  v.tensor_tensor(sl(hr, hsl, TOK), sl(scr2, i, TOK), sl(sce2, i, TOK),
                                  ALU.mult).then_inc(sem_vs, 1))
            v_h = VS.inc()
            free[("scr2", jt % 2)] = (sem_vs, v_h)
            free[("sce2", jt % 2)] = (sem_vs, v_h)
            P.wait("sync", sem_vs, v_h)
            h_st[jt] = dma(P, "sync", semW, ctrW, hsl,
                           hd[jt * 128:(jt + 1) * 128, :], sl(hr, hsl, TOK))
            free[("hr", hsl)] = h_st[jt]

        for k in range(4):
            if ("xre", k) in free:
                P.wait("sync", *free[("xre", k)])
                free.pop(("xre", k))
        for k in range(2):
            P.wait("tensor", *free[("psG", k)])
            P.wait("tensor", *free[("psU", k)])
            free.pop(("psG", k))
            free.pop(("psU", k))
        dn_ps = [ps_g, ps_g, ps_u, ps_u]
        NGT = (CH // 4) * IT
        cast_done = {}

        def dn_wd_dma(gt):
            ob, jt = gt // IT, gt % IT
            slot = gt % 4
            if ("stag", slot) in free:
                P.wait("scalar", *free[("stag", slot)])
            return dma(P, "scalar", semD, ctrD, slot, sl(wdf, slot, 512),
                       wimg["wdob"][ob][:, jt * 512:(jt + 1) * 512])

        def dn_wd_cast(gt, wv):
            slot = gt % 4
            P.wait("scalar", *wv)
            if ("wdsb", slot) in free:
                P.wait("scalar", *free[("wdsb", slot)])
            P.add("scalar", lambda sc, slot=slot:
                  sc.activation(sl(wds, slot, 512), sl(wdf, slot, 512), AF.Identity,
                                bias=0.0, scale=1.0).then_inc(sem_ss, 1))
            cast_done[gt] = SS.inc()
            free[("stag", slot)] = (sem_ss, cast_done[gt])

        wdv = {gt: dn_wd_dma(gt) for gt in range(2)}
        for gt in range(NGT):
            ob, jt = gt // IT, gt % IT
            slot = gt % 4
            if gt + 2 < NGT:
                wdv[gt + 2] = dn_wd_dma(gt + 2)
            dn_wd_cast(gt, wdv.pop(gt))
            hsl = gt % 4
            if ob == 0 and jt < 2 and ("x2a", hsl) in free:
                P.wait("sync", *free[("x2a", hsl)])
            if ("hr", hsl) in free:
                P.wait("sync", *free[("hr", hsl)])
            P.wait("sync", *h_st[jt])
            hv = dma(P, "sync", semA, ctrA, hsl, sl(hr, hsl, TOK),
                     hd[jt * 128:(jt + 1) * 128, :])
            P.wait("tensor", sem_ss, cast_done[gt])
            P.wait("tensor", *hv)
            for oi in range(4):
                if jt == 0 and ("dnps", oi) in free:
                    P.wait("tensor", *free[("dnps", oi)])
                for half in range(2):
                    lastm = (oi == 3 and half == 1)
                    P.add("tensor", lambda te, oi=oi, h2=half, wsl=slot, hsl=hsl,
                          st=(jt == 0), sp=(jt == IT - 1), l=lastm:
                          te.matmul(dn_ps[oi][:, (oi % 2) * TOK + h2 * TT:
                                              (oi % 2) * TOK + h2 * TT + TT],
                                    wds[:, wsl * 512 + oi * 128: wsl * 512 + (oi + 1) * 128],
                                    hr[:, hsl * TOK + h2 * TT: hsl * TOK + h2 * TT + TT],
                                    start=st, stop=sp)
                          .then_inc(sem_ms, 1) if l else
                          te.matmul(dn_ps[oi][:, (oi % 2) * TOK + h2 * TT:
                                              (oi % 2) * TOK + h2 * TT + TT],
                                    wds[:, wsl * 512 + oi * 128: wsl * 512 + (oi + 1) * 128],
                                    hr[:, hsl * TOK + h2 * TT: hsl * TOK + h2 * TT + TT],
                                    start=st, stop=sp))
            if jt == 0:
                x_pre = {}
                for oi in range(2):
                    o = ob * 4 + oi
                    if ("xre2", oi) in free:
                        P.wait("sync", *free[("xre2", oi)])
                    x_pre[oi] = dma(P, "sync", semX, ctrX, oi,
                                    xre[:, oi * TOK:(oi + 1) * TOK],
                                    x2d[o * 128:(o + 1) * 128, :])
            m_gt = MS.inc()
            free[("wdsb", slot)] = (sem_ms, m_gt)
            free[("hr", hsl)] = (sem_ms, m_gt)
            if jt != IT - 1:
                continue
            m_dn = m_gt
            v_scale, v_out = {}, {}

            def dn_scale(oi, ob=ob):
                o = ob * 4 + oi
                esl = o % 2
                P.wait("vector", sem_ms, m_dn)
                if ("dnt", esl) in free:
                    P.wait("vector", *free[("dnt", esl)])
                P.add("vector", lambda v, oi=oi, o=o, i=esl:
                      v.tensor_scalar(sl(dn_t, i, TOK), dn_ps[oi][:, (oi % 2) * TOK:
                                                                  (oi % 2 + 1) * TOK],
                                      b_sb["ad"][:, o:o + 1], None, ALU.mult)
                      .then_inc(sem_vs, 1))
                v_scale[oi] = VS.inc()
                free[("dnps", oi)] = (sem_vs, v_scale[oi])

            def dn_add(oi, x_ld, ob=ob):
                o = ob * 4 + oi
                esl = o % 2
                P.wait("vector", sem_vs, v_scale[oi])
                P.wait("vector", *x_ld)
                if ("dno", esl) in free:
                    P.wait("vector", *free[("dno", esl)])
                P.add("vector", lambda v, i=esl, xs=esl:
                      v.tensor_tensor(sl(dn_o, i, TOK), sl(dn_t, i, TOK),
                                      xre[:, xs * TOK:(xs + 1) * TOK], ALU.add)
                      .then_inc(sem_vs, 1))
                v_out[oi] = VS.inc()
                free[("dnt", esl)] = (sem_vs, v_out[oi])
                free[("xre2", esl)] = (sem_vs, v_out[oi])
                P.wait("sync", sem_vs, v_out[oi])
                o_out = dma(P, "sync", semO, ctrO, o % 2,
                            outT[o * 128:(o + 1) * 128, :], sl(dn_o, esl, TOK))
                free[("dno", esl)] = o_out

            def x_late(oi, ob=ob):
                o = ob * 4 + oi
                esl = o % 2
                P.wait("sync", *free[("xre2", esl)])
                return dma(P, "sync", semX, ctrX, esl,
                           xre[:, esl * TOK:(esl + 1) * TOK],
                           x2d[o * 128:(o + 1) * 128, :])

            dn_scale(0)
            dn_scale(1)
            dn_add(0, x_pre[0])
            x2_ld = x_late(2)
            dn_scale(2)
            dn_add(1, x_pre[1])
            x3_ld = x_late(3)
            dn_scale(3)
            dn_add(2, x2_ld)
            dn_add(3, x3_ld)
        P.add("sync", lambda sy, v0=ctrO[0].v, v1=ctrO[1].v:
              (sy.wait_ge(semO[0], v0), sy.wait_ge(semO[1], v1)))
        with nc.Block() as b4:
            P.emit(b4)

    return nc, _emit_dumps(nc, dumps, locals())


def _emit_dumps(nc, dumps, env):
    if not dumps:
        return []
    names = []
    with nc.semaphore("dmp") as dsem, nc.Block() as bd:
        def _(sy):
            n = 0
            for nm in dumps:
                src = env[nm]
                ap = src if isinstance(src, bass.AP) else src.ap() if hasattr(src, "ap") else src[:]
                shape = list(ap.tensor.shape) if hasattr(ap, "tensor") else list(src.shape)
                dt = ap.dtype
                out = nc.dram_tensor("dump_" + nm, shape, dt, kind="ExternalOutput").ap()
                sy.dma_start(out[:], ap).then_inc(dsem, 16)
                n += 16
                names.append("dump_" + nm)
            sy.wait_ge(dsem, n)
        bd.sync(_)
    return names


_NC_CACHE = {}


def _prep_inputs(inputs):
    x = np.asarray(inputs["L0_attn_hidden_states"], dtype=np.float32)
    xf = np.ascontiguousarray(x.reshape(2 * 4096, H))

    def img(w, nblk, wide):
        w = np.asarray(w, dtype=np.float32)
        return np.ascontiguousarray(
            w.reshape(nblk, 128, wide // 128, 128).transpose(0, 3, 2, 1).reshape(nblk, 128, wide))

    def cols(v, n):
        return np.ascontiguousarray(np.asarray(v, dtype=np.float32).reshape(n, 128).T)

    common = {
        "wq": img(inputs["L0_attn_Wq"], QH, H),
        "wk": img(inputs["L0_attn_Wk"], KVH, H),
        "wv": img(inputs["L0_attn_Wv"], KVH, H),
        "wo": img(inputs["L0_attn_Wo"], CH, H),
        "wg": img(inputs["L0_ffn_Wg"], IT, H),
        "wu": img(inputs["L0_ffn_Wu"], IT, H),
        "wdob": img(inputs["L0_ffn_Wd"], CH, IT * 128)
                .reshape(CH // 4, 4, 128, IT, 128).transpose(0, 2, 3, 1, 4)
                .reshape(CH // 4, 128, IT * 512).copy(),
        "bq": cols(inputs["L0_attn_bq"], QH),
        "bk": cols(inputs["L0_attn_bk"], KVH),
        "bv": cols(inputs["L0_attn_bv"], KVH),
        "bo": cols(inputs["L0_attn_bo"], CH),
        "ag": cols(inputs["L0_ffn_ag"], IT),
        "au": cols(inputs["L0_ffn_au"], IT),
        "ad": cols(inputs["L0_ffn_ad"], CH),
        "ident": np.eye(128, dtype=ml_dtypes.bfloat16),
    }
    in_maps = []
    for c in range(8):
        m = dict(common)
        m["xT"] = np.ascontiguousarray(xf[c * TOK:(c + 1) * TOK].T)
        in_maps.append(m)
    return in_maps


def kernel(**inputs):
    in_maps = _prep_inputs(inputs)
    if "nc" not in _NC_CACHE:
        _NC_CACHE["nc"] = build()[0]
    nc = _NC_CACHE["nc"]
    res = run_bass_kernel_spmd(nc, in_maps, core_ids=list(range(8)))
    parts = [res.results[c]["outT"].T for c in range(8)]
    out = np.concatenate(parts, axis=0).reshape(2, 4096, H)
    return np.ascontiguousarray(out, dtype=np.float32)
